# version 15
# baseline (speedup 1.0000x reference)
import numpy as np
import ml_dtypes
from contextlib import ExitStack

import concourse.bass as bass
import concourse.mybir as mybir
from concourse.bass_utils import run_bass_kernel_spmd

F32 = mybir.dt.float32
BF16 = mybir.dt.bfloat16
AF = mybir.ActivationFunctionType
ALU = mybir.AluOpType
AX = mybir.AxisListType.X

NCORE = 8
D = 2048
DC = 16
I_FFN = 8192
MQ = 16
EXP_BIAS = -20.0
INV_SQRT_HD = float(1.0 / np.sqrt(128.0))
BF = ml_dtypes.bfloat16


_uid = [0]


def _nm(pfx):
    _uid[0] += 1
    return f"{pfx}{_uid[0]}"


class Ctr:
    def __init__(self, handle):
        self.h = handle
        self.n = 0

    def reserve(self, k=1):
        self.n += k
        return self.n


class Phase:
    def __init__(self, nc):
        self.nc = nc
        self.ops = {k: [] for k in ("sync", "tensor", "vector", "scalar", "gpsimd")}

    def y(self, fn):
        self.ops["sync"].append(fn)

    def t(self, fn):
        self.ops["tensor"].append(fn)

    def v(self, fn):
        self.ops["vector"].append(fn)

    def s(self, fn):
        self.ops["scalar"].append(fn)

    def g(self, fn):
        self.ops["gpsimd"].append(fn)

    def run(self):
        with self.nc.Block() as blk:
            for name, deco in (("sync", blk.sync), ("tensor", blk.tensor),
                               ("vector", blk.vector), ("scalar", blk.scalar),
                               ("gpsimd", blk.gpsimd)):
                fns = self.ops[name]
                if fns:
                    def body(e, _fns=tuple(fns)):
                        for f in _fns:
                            f(e)
                    deco(body)


def build_program(BLK=512, QC=256, stage=99):
    S = 16 * BLK
    TOK = 2 * BLK
    NT = TOK // 128
    TH = TOK // 2
    NTH = TH // 128
    NPAT = QC // 128
    NQCG = S // QC
    TSPL = [(o, min(512, TOK - o)) for o in range(0, TOK, 512)]

    nc = bass.Bass("TRN2", target_bir_lowering=False, debug=False)

    x_in = nc.dram_tensor("x_in", [TOK, D], F32, kind="ExternalInput")
    w1t = nc.dram_tensor("w1t", [D, 3 * D], F32, kind="ExternalInput")
    b1c = nc.dram_tensor("b1c", [128, 3 * DC], F32, kind="ExternalInput")
    w2t = nc.dram_tensor("w2t", [D, D], F32, kind="ExternalInput")
    b2r = nc.dram_tensor("b2r", [1, D], F32, kind="ExternalInput")
    wqt = nc.dram_tensor("wqt", [D, 3072], F32, kind="ExternalInput")
    wpt = nc.dram_tensor("wpt", [D, D], F32, kind="ExternalInput")
    wg_d = nc.dram_tensor("wg_d", [32 * 128, 4096], F32, kind="ExternalInput")
    wu_d = nc.dram_tensor("wu_d", [32 * 128, 4096], F32, kind="ExternalInput")
    wd_d = nc.dram_tensor("wd_d", [64 * 128, 2048], F32, kind="ExternalInput")
    cos_d = nc.dram_tensor("cos_d", [512, TOK], BF16, kind="ExternalInput")
    sin_d = nc.dram_tensor("sin_d", [512, TOK], BF16, kind="ExternalInput")
    mask_d = nc.dram_tensor("mask_d", [128, NPAT * 2 * QC], BF16, kind="ExternalInput")
    ident_d = nc.dram_tensor("ident_d", [128, 128], BF16, kind="ExternalInput")
    out_d = nc.dram_tensor("out", [TOK, D], F32, kind="ExternalOutput")

    wgb_d = nc.dram_tensor("wgb_d", [32 * 128, 4096], BF16)
    wub_d = nc.dram_tensor("wub_d", [32 * 128, 4096], BF16)
    wdb_d = nc.dram_tensor("wdb_d", [64 * 128, 2048], BF16)
    ar_i = nc.dram_tensor("ar_i", [128, DC], F32)
    ar_o = nc.dram_tensor("ar_o", [128, DC], F32)
    xg_i = [nc.dram_tensor(f"xg_i{h}", [D, TOK // 2], BF16) for h in range(2)]
    xg_o = [nc.dram_tensor(f"xg_o{h}", [NCORE * D, TOK // 2], BF16)
            for h in range(2)]
    at_i = [nc.dram_tensor(f"at_i{h}", [256, S // 4], BF16) for h in range(4)]
    at_og = nc.dram_tensor("at_og", [4 * NCORE * 256, S // 4], BF16)

    es = ExitStack()
    with es:
        x_sb = es.enter_context(nc.sbuf_tensor("x_sb", [128, NT * D], F32))
        xnf = es.enter_context(nc.sbuf_tensor("xnf", [128, DC * TOK], BF16))
        b1_sb = es.enter_context(nc.sbuf_tensor("b1_sb", [128, 3 * DC], F32))
        ident = es.enter_context(nc.sbuf_tensor("ident", [128, 128], BF16))
        ones_c = es.enter_context(nc.sbuf_tensor("ones_c", [128, 1], BF16))
        ones_r = es.enter_context(nc.sbuf_tensor("ones_r", [1, 128], F32))
        expb = es.enter_context(nc.sbuf_tensor("expb", [128, 1], F32))
        stat = es.enter_context(nc.sbuf_tensor("stat", [128, 8 * NT], F32))
        A_sb = es.enter_context(nc.sbuf_tensor("A_sb", [128, DC], F32))
        sCV = es.enter_context(ExitStack())
        cf32 = sCV.enter_context(nc.sbuf_tensor("cf32", [128, 2 * 1024], F32))

        PE = Ctr(es.enter_context(nc.semaphore("sPE")))
        VE = Ctr(es.enter_context(nc.semaphore("sVE")))
        SE = Ctr(es.enter_context(nc.semaphore("sSE")))
        GE = Ctr(es.enter_context(nc.semaphore("sGE")))
        DW = Ctr(es.enter_context(nc.semaphore("sDW")))
        RS = [Ctr(es.enter_context(nc.semaphore(f"sRS{i}"))) for i in range(11)]
        RG = [Ctr(es.enter_context(nc.semaphore(f"sRG{i}"))) for i in range(3)]
        CVW = [Ctr(es.enter_context(nc.semaphore(f"sCVW{i}"))) for i in range(8)]
        RP = [Ctr(es.enter_context(nc.semaphore(f"sRP{i}"))) for i in range(3)]
        DX = Ctr(es.enter_context(nc.semaphore("sDX")))
        DG = Ctr(es.enter_context(nc.semaphore("sDG")))
        CC = Ctr(es.enter_context(nc.semaphore("sCC")))
        PH = Ctr(es.enter_context(nc.semaphore("sPH")))

        def wge(e, ctr, v):
            if v > 0:
                e.wait_ge(ctr.h, v)

        def barrier():
            ph = Phase(nc)
            tgt = PH.n + 5
            PH.n = tgt

            def mk():
                def f(e):
                    e.drain()
                    e.nop().then_inc(PH.h, 1)
                    e.wait_ge(PH.h, tgt)
                return f
            for name in ph.ops:
                ph.ops[name].append(mk())
            ph.run()

        def emit_output():
            p = Phase(nc)

            def fout(e):
                for tc in range(NT):
                    e.dma_start(out=out_d.ap()[tc * 128:(tc + 1) * 128, :],
                                in_=x_sb[:, tc * D:(tc + 1) * D]).then_inc(DX.h, 16)
                    DX.reserve(16)
                e.wait_ge(DX.h, DX.n)
            p.y(fout)
            p.run()

        conv_items = []
        conv_done = [0]
        conv_ridx = [0]
        rp_cast = {}
        stg_wb = {}
        fresh_pc = {}
        wb_done = {}
        cbuf = {}

        def conv_add_one(kind, b, stg, base, npc, src_fn, cache_d):
            cast_waits = {}
            for pc in range(npc):
                ridx = conv_ridx[0]
                conv_ridx[0] += 1
                sl = ridx % 2
                eng_i = 0

                def item(p, b=b, pc=pc, sl=sl, stg=stg, base=base,
                         eng_i=eng_i, kind=kind, npc=npc,
                         src_fn=src_fn, cache_d=cache_d, cast_waits=cast_waits):
                    _vc = rp_cast.get(sl)

                    def fl(e, b=b, pc=pc, sl=sl, _vc=_vc):
                        if _vc is not None:
                            e.wait_ge(_vc[0].h, _vc[1])
                        e.dma_start(out=cf32[:, sl * 1024:(sl + 1) * 1024],
                                    in_=src_fn(b, pc)).then_inc(RP[sl].h, 16)
                    p.y(fl)
                    RP[sl].reserve(16)
                    vl = RP[sl].n

                    ectr = (SE, VE)[eng_i]
                    ring = cbuf["ring"]

                    def fc(e, pc=pc, sl=sl, base=base, _vl=vl,
                           _wb=stg_wb.get(stg, ()),
                           _pe=cbuf["guard_fn"](stg, pc), ring=ring,
                           eng_i=eng_i):
                        e.wait_ge(RP[sl].h, _vl)
                        for ctr, val in _wb:
                            e.wait_ge(ctr.h, val)
                        wge(e, PE, _pe)
                        dst = ring[:, base + pc * 1024:base + (pc + 1) * 1024]
                        srcv = cf32[:, sl * 1024:(sl + 1) * 1024]
                        if eng_i == 0:
                            e.activation(dst, srcv, AF.Copy).then_inc(SE.h, 1)
                        else:
                            e.tensor_copy(dst, srcv).then_inc(VE.h, 1)
                    (p.s, p.v)[eng_i](fc)
                    v = ectr.reserve()
                    rp_cast[sl] = (ectr, v)
                    cast_waits[ectr] = v
                    fresh_pc[(kind, b, pc)] = (ectr, v)

                    if pc == npc - 1:
                        cvw = CVW[stg if stg < 4 else stg - 6]
                        cw = tuple(cast_waits.items())

                        def fw(e, _cw=cw, b=b, base=base, npc=npc,
                               cache_d=cache_d, ring=ring, cvw=cvw):
                            for ctr, val in _cw:
                                e.wait_ge(ctr.h, val)
                            e.dma_start(
                                out=cache_d.ap()[b * 128:(b + 1) * 128, :],
                                in_=ring[:, base:base + npc * 1024],
                            ).then_inc(cvw.h, 16)
                        p.g(fw)
                        cvw.reserve(16)
                        stg_wb[stg] = ((cvw, cvw.n),)
                        wb_done[(kind, b)] = (cvw, cvw.n)
                conv_items.append(item)

        def conv_pump(p, n, cap=None):
            k = 0
            while conv_items and k < n:
                if cap is not None and conv_done[0] >= cap:
                    return
                conv_items.pop(0)(p)
                conv_done[0] += 1
                k += 1

        GU_ITEMS = 256

        def queue_conv():
            for b in range(32):
                for wi, (src_d, kind, cache) in enumerate(
                        ((wg_d, "g", wgb_d), (wu_d, "u", wub_d))):
                    widx = b * 2 + wi

                    def sf(bb, pc, src_d=src_d):
                        return src_d.ap()[bb * 128:(bb + 1) * 128,
                                          pc * 1024:(pc + 1) * 1024]
                    conv_add_one(kind, b, widx % 3, (widx % 3) * 4096, 4,
                                 sf, cache)

            def sfd(bb, pc):
                return wd_d.ap()[bb * 128:(bb + 1) * 128,
                                 pc * 1024:(pc + 1) * 1024]
            for b in range(64):
                conv_add_one("d", b, 10 + b % 3, (b % 3) * 2048, 2, sfd, wdb_d)

        def emit_ln(p, psD, xn_tok, sq_scr, halves=(0, 1)):
            marks = {}
            tr_half = [0, 0]
            ev_half = [0, 0]
            NH2 = NT // 2
            for hlf in halves:
                lo, hi = hlf * NH2, (hlf + 1) * NH2
                for tc in range(lo, hi):
                    def f(e, tc=tc):
                        e.drain()
                        e.activation(sq_scr[:], x_sb[:, tc * D:(tc + 1) * D],
                                     AF.Square,
                                     accum_out=stat[:, NT + tc:NT + tc + 1])
                        e.drain()
                        e.activation(sq_scr[:], x_sb[:, tc * D:(tc + 1) * D],
                                     AF.Identity,
                                     accum_out=stat[:, tc:tc + 1]).then_inc(SE.h, 1)
                    p.s(f)
                    SE.reserve()
                v_stats = SE.n

                def fc(e, lo=lo, hi=hi, _vs=v_stats):
                    e.wait_ge(SE.h, _vs)
                    e.drain()
                    e.tensor_scalar(stat[:, 2 * NT + lo:2 * NT + hi],
                                    stat[:, lo:hi], 1.0 / D, None, ALU.mult)
                    e.drain()
                    e.tensor_tensor(stat[:, lo:hi],
                                    stat[:, 2 * NT + lo:2 * NT + hi],
                                    stat[:, 2 * NT + lo:2 * NT + hi], ALU.mult)
                    e.tensor_scalar(stat[:, 3 * NT + lo:3 * NT + hi],
                                    stat[:, NT + lo:NT + hi], 1.0 / D,
                                    None, ALU.mult)
                    e.drain()
                    e.tensor_tensor(stat[:, 3 * NT + lo:3 * NT + hi],
                                    stat[:, 3 * NT + lo:3 * NT + hi],
                                    stat[:, lo:hi], ALU.subtract)
                    e.drain()
                    e.tensor_scalar(stat[:, 3 * NT + lo:3 * NT + hi],
                                    stat[:, 3 * NT + lo:3 * NT + hi],
                                    1e-5, None, ALU.add).then_inc(VE.h, 1)
                p.v(fc)
                vvar = VE.reserve()

                def fr(e, lo=lo, hi=hi, _vv=vvar):
                    e.wait_ge(VE.h, _vv)
                    e.drain()
                    e.activation(stat[:, lo:hi],
                                 stat[:, 3 * NT + lo:3 * NT + hi], AF.Ln)
                    e.drain()
                    e.activation(stat[:, 3 * NT + lo:3 * NT + hi],
                                 stat[:, lo:hi], AF.Exp,
                                 scale=-0.5).then_inc(SE.h, 1)
                p.s(fr)
                vr = SE.reserve()

                def fs(e, lo=lo, hi=hi, _vr=vr):
                    e.wait_ge(SE.h, _vr)
                    e.drain()
                    e.tensor_scalar(stat[:, 4 * NT + lo:4 * NT + hi],
                                    stat[:, 3 * NT + lo:3 * NT + hi], 1.0,
                                    None, ALU.add)
                    e.tensor_tensor(stat[:, 5 * NT + lo:5 * NT + hi],
                                    stat[:, 2 * NT + lo:2 * NT + hi],
                                    stat[:, 3 * NT + lo:3 * NT + hi], ALU.mult)
                    e.drain()
                    e.tensor_scalar(stat[:, 5 * NT + lo:5 * NT + hi],
                                    stat[:, 5 * NT + lo:5 * NT + hi], -1.0,
                                    None, ALU.mult).then_inc(VE.h, 1)
                p.v(fs)
                vs12 = VE.reserve()

                for tc in range(lo, hi):
                    half = tc % 2
                    xv = xn_tok[:, half * D:(half + 1) * D]

                    def fn(e, tc=tc, xv=xv, _vs=vs12, _pe=tr_half[half]):
                        e.wait_ge(VE.h, _vs)
                        wge(e, PE, _pe)
                        e.drain()
                        e.tensor_scalar(
                            xv, x_sb[:, tc * D:(tc + 1) * D],
                            stat[:, 2 * NT + tc:2 * NT + tc + 1],
                            stat[:, 3 * NT + tc:3 * NT + tc + 1],
                            ALU.subtract, ALU.mult).then_inc(VE.h, 1)
                    p.v(fn)
                    vn = VE.reserve()

                    pA, pB = psD[2 * half], psD[2 * half + 1]

                    def ft(e, xv=xv, pA=pA, pB=pB, _vn=vn, _ve=ev_half[half]):
                        e.wait_ge(VE.h, _vn)
                        wge(e, VE, _ve)
                        ins = None
                        for dc in range(16):
                            pb = (pA if dc < 8 else pB).ap().bitcast(BF16)
                            ins = e.transpose(pb[:, (dc % 8) * 128:(dc % 8 + 1) * 128],
                                              xv[:, dc * 128:(dc + 1) * 128], ident[:])
                        ins.then_inc(PE.h, 1)
                    p.t(ft)
                    tr_half[half] = PE.reserve()

                    def fe(e, tc=tc, pA=pA, pB=pB, _vt=tr_half[half]):
                        e.wait_ge(PE.h, _vt)
                        e.drain()
                        for b, pp in enumerate((pA, pB)):
                            pb = pp.ap().bitcast(BF16)
                            e.tensor_copy(
                                bass.AP(xnf, (b * 8) * TOK + tc * 128,
                                        [[DC * TOK, 128], [TOK, 8], [1, 128]]),
                                pb[:, 0:1024].rearrange("p (c k) -> p c k", c=8))
                        e.drain()
                        e.nop().then_inc(VE.h, 1)
                    p.v(fe)
                    ev_half[half] = VE.reserve()
                marks[hlf] = VE.n
            return marks

        ln_ctr = [0]

        def ln_phase(hook=None, halves=(0, 1)):
            ln_ctr[0] += 1
            with ExitStack() as sL:
                xn_tok = sL.enter_context(
                    nc.sbuf_tensor(_nm("xn_tok"), [128, 2 * D], BF16))
                sq_l = sL.enter_context(nc.sbuf_tensor(_nm("sq"), [128, D], BF16))
                psD = [sL.enter_context(nc.psum_tensor(_nm("ps"), [128, 512], F32))
                       for _ in range(4)]
                p = Phase(nc)
                marks = emit_ln(p, psD, xn_tok, sq_l, halves=halves)
                if hook is not None:
                    hook(p, marks)
                p.run()
                barrier()

        p = Phase(nc)
        for tc in range(NT):
            def f(e, tc=tc):
                e.dma_start(out=x_sb[:, tc * D:(tc + 1) * D],
                            in_=x_in.ap()[tc * 128:(tc + 1) * 128, :]).then_inc(DX.h, 16)
            p.y(f)
            DX.reserve(16)
        for dst, src in [(b1_sb, b1c), (ident, ident_d)]:
            def f(e, dst=dst, src=src):
                e.dma_start(out=dst[:], in_=src.ap()).then_inc(DX.h, 16)
            p.y(f)
            DX.reserve(16)
        vload = DX.n

        def fv(e, _v=vload):
            e.wait_ge(DX.h, _v)
            e.memset(ones_c[:], 1.0)
            e.memset(ones_r[:], 1.0)
            e.memset(expb[:], EXP_BIAS)
            e.nop().then_inc(VE.h, 1)
        p.v(fv)
        VE.reserve()
        p.run()
        barrier()

        ln_phase()
        if stage <= 1:
            emit_output()
            return nc

        with ExitStack() as sA:
            kq = sA.enter_context(nc.sbuf_tensor("kq", [128, DC * TOK], BF16))
            sA2 = sA.enter_context(ExitStack())
            cos_sb = sA2.enter_context(nc.sbuf_tensor("cos_sb", [128, 4 * TOK], BF16))
            sin_sb = sA2.enter_context(nc.sbuf_tensor("sin_sb", [128, 4 * TOK], BF16))
            esav = sA2.enter_context(nc.sbuf_tensor("esav", [128, TOK], BF16))
            rscr = sA2.enter_context(nc.sbuf_tensor("rscr", [128, TOK], BF16))
            vch = sA2.enter_context(nc.sbuf_tensor("vch", [128, 2 * TOK], BF16))
            vscr = sA2.enter_context(nc.sbuf_tensor("vscr", [128, TOK], BF16))
            wb_sb = sA2.enter_context(nc.sbuf_tensor("wb_sb", [128, TOK], BF16))
            nrm = sA2.enter_context(nc.sbuf_tensor("nrm", [1, TOK], F32))
            w1f = sA2.enter_context(nc.sbuf_tensor("w1f", [128, 2 * 1024], F32))
            w1b = sA2.enter_context(nc.sbuf_tensor("w1b", [128, 2 * DC * 128], BF16))
            sq_scr = sA2.enter_context(nc.sbuf_tensor("sq_a", [128, D], BF16))
            psQ = [sA2.enter_context(nc.psum_tensor(_nm("ps"), [128, 512], F32))
                   for _ in range(2 * len(TSPL))]
            psN = [sA2.enter_context(nc.psum_tensor(_nm("ps"), [1, 512], F32))
                   for _ in range(len(TSPL))]
            psW = [sA2.enter_context(nc.psum_tensor(_nm("ps"), [128, 512], F32))
                   for _ in range(len(TSPL))]

            def rope_inplace(e, slot_t):
                for q in range(4):
                    E = slot_t[:, q * TOK:(q + 1) * TOK]
                    O = slot_t[:, (4 + q) * TOK:(5 + q) * TOK]
                    Ct = cos_sb[:, q * TOK:(q + 1) * TOK]
                    St = sin_sb[:, q * TOK:(q + 1) * TOK]
                    e.drain()
                    e.tensor_copy(esav[:], E)
                    e.tensor_tensor(rscr[:], O, St, ALU.mult)
                    e.drain()
                    e.tensor_tensor(E, E, Ct, ALU.mult)
                    e.drain()
                    e.tensor_tensor(E, E, rscr[:], ALU.subtract)
                    e.tensor_tensor(O, O, Ct, ALU.mult)
                    e.drain()
                    e.tensor_tensor(rscr[:], esav[:], St, ALU.mult)
                    e.drain()
                    e.tensor_tensor(O, O, rscr[:], ALU.add)
                    e.drain()

            p = Phase(nc)

            def f_tab(e):
                e.dma_start(
                    out=cos_sb[:].rearrange("p (c t) -> p c t", c=4),
                    in_=cos_d.ap().rearrange("(c p) t -> p c t", c=4, p=128),
                ).then_inc(DX.h, 16)
                e.dma_start(
                    out=sin_sb[:].rearrange("p (c t) -> p c t", c=4),
                    in_=sin_d.ap().rearrange("(c p) t -> p c t", c=4, p=128),
                ).then_inc(DX.h, 16)
            p.y(f_tab)
            DX.reserve(32)
            v_tab = DX.n

            ec_order = list(range(16, 48)) + list(range(0, 16))
            cast_v = {}
            mm_v = {}
            evac_v = {}
            z_v = {}
            v_kro = v_qro = v_wb = v_A_done = 0

            def emit_w1(i, ec):
                slot = i % 2
                for half in range(2):
                    hslot = (i * 2 + half) % 2

                    def fd(e, ec=ec, half=half, hslot=hslot,
                           _vc=cast_v.get(("h", hslot), 0)):
                        wge(e, SE, _vc)
                        for qq in range(2):
                            e.dma_start(
                                out=w1f[:, hslot * 1024 + qq * 512:
                                        hslot * 1024 + (qq + 1) * 512]
                                .rearrange("p (c k) -> p c k", c=4),
                                in_=bass.AP(
                                    w1t,
                                    ((half * 8 + qq * 4) * 128) * (3 * D) + ec * 128,
                                    [[3 * D, 128], [128 * 3 * D, 4], [1, 128]]),
                            ).then_inc(RS[6 + hslot].h, 16)
                    p.y(fd)
                    RS[6 + hslot].reserve(32)
                    v_dma = RS[6 + hslot].n

                    def fcast(e, slot=slot, half=half, hslot=hslot, _vd=v_dma,
                              _vm=mm_v.get(slot, 0)):
                        e.wait_ge(RS[6 + hslot].h, _vd)
                        wge(e, PE, _vm)
                        e.drain()
                        e.activation(
                            w1b[:, slot * DC * 128 + half * 1024:
                                slot * DC * 128 + (half + 1) * 1024],
                            w1f[:, hslot * 1024:(hslot + 1) * 1024],
                            AF.Copy).then_inc(SE.h, 1)
                    p.s(fcast)
                    cast_v[("h", hslot)] = SE.reserve()
                cast_v[slot] = cast_v[("h", (i * 2 + 1) % 2)]

            emit_w1(0, ec_order[0])
            for i, ec in enumerate(ec_order):
                slot = i % 2
                part, sub = ec // DC, ec % DC
                par = i % 2
                pidx = tuple(par * len(TSPL) + j for j in range(len(TSPL)))

                def fmm(e, slot=slot, _vc=cast_v[slot], _ve=evac_v.get(par, 0),
                        pidx=pidx):
                    e.wait_ge(SE.h, _vc)
                    wge(e, SE, _ve)
                    ins = None
                    for j, (toff, tw) in enumerate(TSPL):
                        pt = psQ[pidx[j]]
                        for dc in range(DC):
                            ins = e.matmul(
                                pt[:, 0:tw],
                                w1b[:, slot * DC * 128 + dc * 128:
                                    slot * DC * 128 + (dc + 1) * 128],
                                xnf[:, dc * TOK + toff:dc * TOK + toff + tw],
                                start=(dc == 0), stop=(dc == DC - 1))
                    ins.then_inc(PE.h, 1)
                p.t(fmm)
                vmm = PE.reserve()
                mm_v[slot] = vmm
                if i + 1 < len(ec_order):
                    emit_w1(i + 1, ec_order[i + 1])

                if part == 2:
                    dst, doff = vch, (sub % 2) * TOK
                else:
                    dst, doff = kq, sub * TOK
                vz_wait = z_v.get(sub % 2, 0) if part == 2 else 0
                vq_wait = v_A_done if part == 0 else 0

                def fev(e, ec=ec, dst=dst, doff=doff, _vm=vmm, pidx=pidx,
                        _vz=vz_wait, _vq=vq_wait):
                    e.wait_ge(PE.h, _vm)
                    wge(e, VE, _vz)
                    wge(e, VE, _vq)
                    e.drain()
                    ins = None
                    for j, (toff, tw) in enumerate(TSPL):
                        ins = e.activation(
                            dst[:, doff + toff:doff + toff + tw],
                            psQ[pidx[j]][:, 0:tw],
                            AF.Identity, bias=b1_sb[:, ec:ec + 1])
                    ins.then_inc(SE.h, 1)
                p.s(fev)
                vev = SE.reserve()
                evac_v[par] = vev

                if part == 1 and sub == 7:
                    def frope(e, _ve=vev, _vt=v_tab):
                        e.wait_ge(SE.h, _ve)
                        e.wait_ge(DX.h, _vt)
                        rope_inplace(e, kq)
                        e.nop().then_inc(VE.h, 1)
                    p.v(frope)
                    v_kro = VE.reserve()

                if part == 1 and sub == 15:
                    v_prev_nm = 0
                    for dc in range(DC):
                        def fsq(e, dc=dc, _vk=vev, _vr=v_kro, _vp=v_prev_nm):
                            e.wait_ge(SE.h, _vk)
                            e.wait_ge(VE.h, _vr)
                            wge(e, PE, _vp)
                            e.drain()
                            e.activation(sq_scr[:, 0:TOK],
                                         kq[:, dc * TOK:(dc + 1) * TOK],
                                         AF.Square).then_inc(SE.h, 1)
                        p.s(fsq)
                        vsq = SE.reserve()

                        def fnm(e, dc=dc, _vs=vsq):
                            e.wait_ge(SE.h, _vs)
                            ins = None
                            for j, (toff, tw) in enumerate(TSPL):
                                ins = e.matmul(psN[j][:, 0:tw], ones_c[:],
                                               sq_scr[:, toff:toff + tw],
                                               start=(dc == 0), stop=(dc == DC - 1))
                            ins.then_inc(PE.h, 1)
                        p.t(fnm)
                        v_prev_nm = PE.reserve()

                    def fw(e, _vp=v_prev_nm):
                        e.wait_ge(PE.h, _vp)
                        e.drain()
                        for j, (toff, tw) in enumerate(TSPL):
                            e.tensor_scalar(nrm[:, toff:toff + tw], psN[j][:, 0:tw],
                                            1e-24, None, ALU.max)
                        e.drain()
                        e.nop().then_inc(VE.h, 1)
                    p.v(fw)
                    v_w1 = VE.reserve()

                    def fw2(e, _v=v_w1):
                        e.wait_ge(VE.h, _v)
                        e.drain()
                        e.activation(nrm[:], nrm[:], AF.Ln)
                        e.drain()
                        e.activation(nrm[:], nrm[:], AF.Exp, scale=-0.5).then_inc(SE.h, 1)
                    p.s(fw2)
                    v_w2 = SE.reserve()

                    def fw3(e, _v=v_w2):
                        e.wait_ge(SE.h, _v)
                        ins = None
                        for j, (toff, tw) in enumerate(TSPL):
                            ins = e.matmul(psW[j][:, 0:tw], ones_r[:],
                                           nrm[:, toff:toff + tw],
                                           start=True, stop=True)
                        ins.then_inc(PE.h, 1)
                    p.t(fw3)
                    v_w3 = PE.reserve()

                    def fw4(e, _v=v_w3):
                        e.wait_ge(PE.h, _v)
                        e.drain()
                        for j, (toff, tw) in enumerate(TSPL):
                            e.tensor_copy(wb_sb[:, toff:toff + tw], psW[j][:, 0:tw])
                        e.drain()
                        e.nop().then_inc(VE.h, 1)
                    p.v(fw4)
                    v_wb = VE.reserve()

                if part == 2:
                    def fz(e, sub=sub, doff=doff, _ve=vev, _vw=v_wb):
                        e.wait_ge(SE.h, _ve)
                        e.wait_ge(VE.h, _vw)
                        e.drain()
                        e.tensor_tensor(vscr[:], vch[:, doff:doff + TOK], wb_sb[:],
                                        ALU.mult)
                        e.drain()
                        e.tensor_tensor(vscr[:], vscr[:],
                                        kq[:, sub * TOK:(sub + 1) * TOK], ALU.mult)
                        e.drain()
                        e.reduce_sum(A_sb[:, sub:sub + 1], vscr[:], axis=AX)
                        e.drain()
                        e.nop().then_inc(VE.h, 1)
                    p.v(fz)
                    z_v[sub % 2] = VE.reserve()
                    if sub == DC - 1:
                        v_A_done = z_v[sub % 2]

                if part == 0 and sub == 7:
                    def frq(e, _ve=vev, _vt=v_tab):
                        e.wait_ge(SE.h, _ve)
                        e.wait_ge(DX.h, _vt)
                        rope_inplace(e, kq)
                        e.nop().then_inc(VE.h, 1)
                    p.v(frq)
                    v_qro = VE.reserve()

            v_q_all = SE.n

            def fa1(e, _va=v_A_done):
                e.wait_ge(VE.h, _va)
                e.dma_start(out=ar_i.ap(), in_=A_sb[:]).then_inc(DG.h, 16)
            p.g(fa1)
            DG.reserve(16)
            vdg1 = DG.n

            def fa2(e, _v=vdg1):
                e.wait_ge(DG.h, _v)
                e.collective_compute(
                    "AllReduce", ALU.add, replica_groups=[list(range(NCORE))],
                    ins=[ar_i.ap()], outs=[ar_o.ap()]).then_inc(CC.h, 1)
            p.g(fa2)
            CC.reserve(1)
            vcc = CC.n

            def fa3(e, _v=vcc):
                e.wait_ge(CC.h, _v)
                e.dma_start(out=A_sb[:], in_=ar_o.ap()).then_inc(DG.h, 16)
            p.g(fa3)
            DG.reserve(16)
            v_A_red = DG.n

            def fy(e, _va=v_A_red, _vq=v_qro, _vqa=v_q_all):
                e.wait_ge(DG.h, _va)
                e.wait_ge(VE.h, _vq)
                e.wait_ge(SE.h, _vqa)
                e.drain()
                for dc in range(DC):
                    e.tensor_scalar(kq[:, dc * TOK:(dc + 1) * TOK],
                                    kq[:, dc * TOK:(dc + 1) * TOK],
                                    A_sb[:, dc:dc + 1], None, ALU.mult)
                e.drain()
                e.nop().then_inc(VE.h, 1)
            p.v(fy)
            v_y = VE.reserve()
            p.run()
            barrier()
            sA2.close()

            with ExitStack() as sW:
                wtf = sW.enter_context(nc.sbuf_tensor("wtf", [128, 4 * 512], F32))
                wtb = sW.enter_context(nc.sbuf_tensor("wtb", [128, 4 * 512], BF16))
                s_scr = sW.enter_context(nc.sbuf_tensor("s_scr", [128, 512], F32))
                b2f = sW.enter_context(nc.sbuf_tensor("b2f", [1, D], F32))
                psO = [sW.enter_context(nc.psum_tensor(_nm("ps"), [128, 512], F32))
                       for _ in range(NT)]
                p = Phase(nc)

                def fb2(e):
                    e.dma_start(out=b2f[:], in_=b2r.ap()).then_inc(DX.h, 16)
                p.y(fb2)
                DX.reserve(16)
                v_b2 = DX.n
                t_cast = {}
                t_mm = {}
                v_evac = 0
                for et in range(4):
                    v_last_cast = 0
                    for dc in range(DC):
                        slot = (et * DC + dc) % 4

                        def fd(e, et=et, dc=dc, slot=slot, _vc=t_cast.get(slot, 0)):
                            wge(e, SE, _vc)
                            e.dma_start(
                                out=wtf[:, slot * 512:(slot + 1) * 512],
                                in_=w2t.ap()[dc * 128:(dc + 1) * 128,
                                             et * 512:(et + 1) * 512],
                            ).then_inc(RS[slot].h, 16)
                        p.y(fd)
                        RS[slot].reserve(16)
                        vd = RS[slot].n

                        def fc(e, slot=slot, _vd=vd, _vm=t_mm.get(slot, 0)):
                            e.wait_ge(RS[slot].h, _vd)
                            wge(e, PE, _vm)
                            e.drain()
                            e.activation(wtb[:, slot * 512:(slot + 1) * 512],
                                         wtf[:, slot * 512:(slot + 1) * 512],
                                         AF.Copy).then_inc(SE.h, 1)
                        p.s(fc)
                        t_cast[slot] = SE.reserve()
                        v_last_cast = t_cast[slot]

                        def fmm(e, et=et, dc=dc, slot=slot, _vc=t_cast[slot],
                                _vy=v_y, _ve=v_evac):
                            e.wait_ge(SE.h, _vc)
                            e.wait_ge(VE.h, _vy)
                            if dc == 0:
                                wge(e, VE, _ve)
                            ins = None
                            for tc in range(NT):
                                ins = e.matmul(
                                    psO[tc][:, 0:512],
                                    kq[:, dc * TOK + tc * 128:dc * TOK + (tc + 1) * 128],
                                    wtb[:, slot * 512:(slot + 1) * 512],
                                    start=(dc == 0), stop=False)
                            ins.then_inc(PE.h, 1)
                        p.t(fmm)
                        t_mm[slot] = PE.reserve()

                    def fbias(e, et=et, _vb=v_b2):
                        e.wait_ge(DX.h, _vb)
                        ins = None
                        for tc in range(NT):
                            ins = e.matmul(psO[tc][:, 0:512], ones_r[:],
                                           b2f[:, et * 512:(et + 1) * 512],
                                           start=False, stop=True)
                        ins.then_inc(PE.h, 1)
                    p.t(fbias)
                    vb = PE.reserve()

                    def fev(e, et=et, _vb=vb):
                        e.wait_ge(PE.h, _vb)
                        for tc in range(NT):
                            e.drain()
                            e.tensor_scalar(
                                s_scr[:],
                                x_sb[:, tc * D + et * 512:tc * D + (et + 1) * 512],
                                stat[:, 4 * NT + tc:4 * NT + tc + 1],
                                stat[:, 5 * NT + tc:5 * NT + tc + 1],
                                ALU.mult, ALU.add)
                            e.drain()
                            e.tensor_tensor(s_scr[:], s_scr[:], psO[tc][:, 0:512],
                                            ALU.add)
                            e.drain()
                            e.tensor_copy(
                                x_sb[:, tc * D + et * 512:tc * D + (et + 1) * 512],
                                s_scr[:])
                        e.drain()
                        e.nop().then_inc(VE.h, 1)
                    p.v(fev)
                    v_evac = VE.reserve()

                p.run()
            barrier()

        ffn_ctr = [0]

        def ffn_pass(mid=None):
            ffn_ctr[0] += 1
            fp = ffn_ctr[0]
            with ExitStack() as sF:
                hs = sF.enter_context(nc.sbuf_tensor(_nm("hs"), [128, 64 * TH], BF16))
                wring = sF.enter_context(nc.sbuf_tensor(_nm("wring"), [128, 3 * 4096], BF16))
                gur = wring
                wdr = wring
                hb = sF.enter_context(nc.sbuf_tensor(_nm("hb"), [128, 4 * TH], BF16))
                tre = sF.enter_context(nc.sbuf_tensor(_nm("tre"), [128, 3 * TH], BF16))
                sg = sF.enter_context(nc.sbuf_tensor(_nm("sg"), [128, 2 * TH], BF16))
                psF = [sF.enter_context(nc.psum_tensor(_nm("ps"), [128, 512], F32))
                       for _ in range(8)]
                Z16 = [0] * 16

                for th in range(2):
                    fresh = (fp == 1 and th == 0)
                    p = Phase(nc)
                    if fp == 1 and th == 1:
                        def fhead(e, _f=tuple((CVW[i], CVW[i].n)
                                              for i in range(8))):
                            for ctr, val in _f:
                                e.wait_ge(ctr.h, val)
                        p.y(fhead)
                    widx = 0
                    wmm_v = {}
                    wmm_hc = {}
                    gevac_v = {}
                    spars_v = 0
                    if fresh:
                        cbuf["ring"] = wring
                        cbuf["guard_fn"] = \
                            lambda stg, pc: wmm_hc.get(stg, Z16)[4 * pc + 3]
                        conv_pump(p, 12, cap=GU_ITEMS)
                    for m in range(MQ):
                        for jp in range(2):
                            gidx = m * 2 + jp
                            b0 = 4 * (gidx % 2)
                            blk = gidx
                            slots = {}
                            waitsets = {}
                            for wi, (cache_d, kp) in enumerate(
                                    ((wgb_d, "g"), (wub_d, "u"))):
                                slot = widx % 3
                                widx += 1
                                slots[wi] = slot
                                if fresh:
                                    waitsets[wi] = ()
                                    continue

                                ohc = wmm_hc.get(slot, Z16)
                                halves = []
                                for hf in range(2):
                                    sem = RS[slot] if hf == 0 else RS[8 + slot]

                                    def fd(e, blk=blk, slot=slot, hf=hf,
                                           cache_d=cache_d, sem=sem,
                                           _vm=ohc[8 * hf + 7],
                                           _wb=wb_done[(kp, blk)]):
                                        wge(e, PE, _vm)
                                        e.wait_ge(_wb[0].h, _wb[1])
                                        e.dma_start(
                                            out=gur[:, slot * 4096 + hf * 2048:
                                                slot * 4096 + (hf + 1) * 2048],
                                            in_=cache_d.ap()[blk * 128:(blk + 1) * 128,
                                                             hf * 2048:(hf + 1) * 2048],
                                        ).then_inc(sem.h, 16)
                                    p.y(fd)
                                    sem.reserve(16)
                                    halves.append((sem, sem.n))
                                waitsets[wi] = tuple(halves)

                            hcl = []
                            for hc in range(16):
                                if fresh:
                                    fw_hc = (fresh_pc[("g", blk, hc // 4)],
                                             fresh_pc[("u", blk, hc // 4)])
                                else:
                                    fw_hc = ()

                                if fresh:
                                    hw = ()
                                elif hc in (0, 8):
                                    hw = (waitsets[0][hc // 8],
                                          waitsets[1][hc // 8])
                                else:
                                    hw = ()

                                def fmm(e, hc=hc, th=th, b0=b0,
                                        sg_=slots[0], su_=slots[1],
                                        _fw=fw_hc, _hw=hw,
                                        _vev=gevac_v.get(gidx % 2, 0)):
                                    for ctr, val in _fw:
                                        e.wait_ge(ctr.h, val)
                                    for ctr, val in _hw:
                                        e.wait_ge(ctr.h, val)
                                    if hc == 0:
                                        wge(e, VE, _vev)
                                    ins = None
                                    for jj in range(2):
                                        e.matmul(
                                            psF[b0 + jj][:, 0:TH],
                                            gur[:, sg_ * 4096 + hc * 256 + jj * 128:
                                                sg_ * 4096 + hc * 256 + (jj + 1) * 128],
                                            xnf[:, hc * TOK + th * TH:
                                                hc * TOK + (th + 1) * TH],
                                            start=(hc == 0), stop=(hc == 15))
                                        ins = e.matmul(
                                            psF[b0 + 2 + jj][:, 0:TH],
                                            gur[:, su_ * 4096 + hc * 256 + jj * 128:
                                                su_ * 4096 + hc * 256 + (jj + 1) * 128],
                                            xnf[:, hc * TOK + th * TH:
                                                hc * TOK + (th + 1) * TH],
                                            start=(hc == 0), stop=(hc == 15))
                                    ins.then_inc(PE.h, 1)
                                p.t(fmm)
                                vmm = PE.reserve()
                                hcl.append(vmm)
                            wmm_v[slots[0]] = vmm
                            wmm_v[slots[1]] = vmm
                            wmm_hc[slots[0]] = hcl
                            wmm_hc[slots[1]] = hcl
                            v_group_mm = vmm
                            if fresh:
                                conv_pump(p, 8, cap=GU_ITEMS)

                            def fsilu(e, b0=b0, jp=jp, _vm=v_group_mm,
                                      _vh=gevac_v.get((gidx - 1) % 2, 0),
                                      _vsp=spars_v):
                                e.wait_ge(PE.h, _vm)
                                wge(e, VE, _vh)
                                wge(e, VE, _vsp)
                                e.drain()
                                for jj in range(2):
                                    e.activation(sg[:, jj * TH:(jj + 1) * TH],
                                                 psF[b0 + jj][:, 0:TH],
                                                 AF.Silu).then_inc(SE.h, 1)
                            p.s(fsilu)
                            SE.reserve(2)
                            v_silu = SE.n

                            def fh(e, b0=b0, jp=jp, _vs=v_silu, _vsp=spars_v):
                                e.wait_ge(SE.h, _vs)
                                wge(e, VE, _vsp)
                                e.drain()
                                for jj in range(2):
                                    j = 2 * jp + jj
                                    e.tensor_tensor(hb[:, j * TH:(j + 1) * TH],
                                                    sg[:, jj * TH:(jj + 1) * TH],
                                                    psF[b0 + 2 + jj][:, 0:TH],
                                                    ALU.mult)
                                e.drain()
                                e.nop().then_inc(VE.h, 1)
                            p.v(fh)
                            gevac_v[gidx % 2] = VE.reserve()

                        v_h_all = gevac_v[(m * 2 + 1) % 2]

                        def fsparse(e, m=m, _vh=v_h_all):
                            e.wait_ge(VE.h, _vh)
                            h0, h1 = hb[:, 0:TH], hb[:, TH:2 * TH]
                            h2, h3 = hb[:, 2 * TH:3 * TH], hb[:, 3 * TH:4 * TH]
                            T0, T1 = sg[:, 0:TH], sg[:, TH:2 * TH]
                            M01, N01 = tre[:, 0:TH], tre[:, TH:2 * TH]
                            SEC = tre[:, 2 * TH:3 * TH]
                            e.drain()
                            e.tensor_tensor(T0, h0, h0, ALU.mult)
                            e.tensor_tensor(T1, h1, h1, ALU.mult)
                            e.drain()
                            e.tensor_tensor(M01, T0, T1, ALU.max)
                            e.tensor_tensor(N01, T0, T1, ALU.min)
                            e.drain()
                            e.tensor_tensor(T0, h2, h2, ALU.mult)
                            e.tensor_tensor(T1, h3, h3, ALU.mult)
                            e.drain()
                            e.tensor_tensor(SEC, T0, T1, ALU.max)
                            e.drain()
                            e.tensor_tensor(T0, T0, T1, ALU.min)
                            e.drain()
                            e.tensor_tensor(T1, M01, SEC, ALU.min)
                            e.tensor_tensor(T0, N01, T0, ALU.max)
                            e.drain()
                            e.tensor_tensor(SEC, T1, T0, ALU.max)
                            e.drain()
                            for j in range(4):
                                hj = hb[:, j * TH:(j + 1) * TH]
                                e.tensor_tensor(T0, hj, hj, ALU.mult)
                                e.drain()
                                e.tensor_tensor(T0, T0, SEC, ALU.is_ge)
                                e.drain()
                                e.tensor_tensor(hs[:, (4 * m + j) * TH:
                                                   (4 * m + j + 1) * TH],
                                                hj, T0, ALU.mult)
                                e.drain()
                            e.nop().then_inc(VE.h, 1)
                        p.v(fsparse)
                        spars_v = VE.reserve()
                    p.run()
                    barrier()

                    p = Phase(nc)
                    d_mm = {}
                    v_devac = 0
                    if fresh:
                        for s_ in range(3):
                            stg_wb[10 + s_] = stg_wb.get(s_ // 2, ())
                        cbuf["guard_fn"] = lambda stg, pc: d_mm.get(stg, 0)
                        conv_pump(p, 6)
                    for Ht in range(4):
                        for g4 in range(16):
                            blk = Ht * 16 + g4
                            slot = blk % 3
                            key = 10 + slot
                            if fresh:
                                ws = (fresh_pc[("d", blk, 0)],
                                      fresh_pc[("d", blk, 1)])
                            else:
                                def fd(e, blk=blk, slot=slot, key=key,
                                       _vm=d_mm.get(key, 0),
                                       _wb=wb_done[("d", blk)]):
                                    wge(e, PE, _vm)
                                    e.wait_ge(_wb[0].h, _wb[1])
                                    e.dma_start(
                                        out=wdr[:, slot * 2048:(slot + 1) * 2048],
                                        in_=wdb_d.ap()[blk * 128:(blk + 1) * 128, :],
                                    ).then_inc(RS[4 + slot].h, 16)
                                p.y(fd)
                                RS[4 + slot].reserve(16)
                                ws = ((RS[4 + slot], RS[4 + slot].n),)

                            def fmm(e, g4=g4, slot=slot, _vd=ws,
                                    _ve=v_devac):
                                for ctr, val in _vd:
                                    e.wait_ge(ctr.h, val)
                                if g4 == 0:
                                    wge(e, VE, _ve)
                                ins = None
                                for icl in range(4):
                                    ic = g4 * 4 + icl
                                    for tcl in range(NTH):
                                        ins = e.matmul(
                                            psF[tcl][:, 0:512],
                                            hs[:, ic * TH + tcl * 128:
                                               ic * TH + (tcl + 1) * 128],
                                            wdr[:, slot * 2048 + icl * 512:
                                                slot * 2048 + (icl + 1) * 512],
                                            start=(ic == 0), stop=(ic == 63))
                                ins.then_inc(PE.h, 1)
                            p.t(fmm)
                            d_mm[key] = PE.reserve()
                            if fresh:
                                conv_pump(p, 2)
                        v_dn = d_mm[key]

                        def fev(e, Ht=Ht, th=th, _vm=v_dn):
                            e.wait_ge(PE.h, _vm)
                            e.drain()
                            for tcl in range(NTH):
                                tc = th * NTH + tcl
                                e.tensor_tensor(
                                    x_sb[:, tc * D + Ht * 512:tc * D + (Ht + 1) * 512],
                                    x_sb[:, tc * D + Ht * 512:tc * D + (Ht + 1) * 512],
                                    psF[tcl][:, 0:512], ALU.add)
                                e.drain()
                            e.nop().then_inc(VE.h, 1)
                        p.v(fev)
                        v_devac = VE.reserve()
                    p.run()
                    barrier()
                    if mid is not None and th == 0:
                        mid(psF)

        if stage <= 2:
            emit_output()
            return nc
        cc_xg = [0, 0]
        cc_at = [0, 0, 0, 0]

        def xn3_emit(p, h, _vmk):
            if True:
                def fx1(e, h=h, _vm=_vmk):
                    e.wait_ge(VE.h, _vm)
                    for qq in range(4):
                        e.dma_start(
                            out=bass.AP(xg_i[h], (qq * 4 * 128) * (TOK // 2),
                                        [[TOK // 2, 128],
                                         [128 * (TOK // 2), 4],
                                         [1, TOK // 2]]),
                            in_=bass.AP(xnf, qq * 4 * TOK + h * (TOK // 2),
                                        [[DC * TOK, 128], [TOK, 4],
                                         [1, TOK // 2]]),
                        ).then_inc(DG.h, 16)
                p.g(fx1)
                DG.reserve(64)
                vx = DG.n

                def fx2(e, h=h, _v=vx):
                    e.wait_ge(DG.h, _v)
                    e.collective_compute(
                        "AllGather", ALU.bypass,
                        replica_groups=[list(range(NCORE))],
                        ins=[xg_i[h].ap()],
                        outs=[xg_o[h].ap()]).then_inc(CC.h, 1)
                p.g(fx2)
                CC.reserve(1)
                cc_xg[h] = CC.n


        ln_phase()

        def ln3h0_mid(psF_):
            with ExitStack() as sL:
                xn_tok = cf32.ap().bitcast(BF16)
                sq_l = sL.enter_context(nc.sbuf_tensor(_nm("sq"), [128, D], BF16))
                p = Phase(nc)
                marks = emit_ln(p, psF_[4:8], xn_tok, sq_l, halves=(0,))
                xn3_emit(p, 0, marks[0])
                p.run()
                barrier()

        queue_conv()
        ffn_pass(mid=ln3h0_mid)
        assert not conv_items, f"conversion queue not drained: {len(conv_items)}"
        sCV.close()
        if stage <= 3:
            emit_output()
            return nc
        ln_phase(hook=lambda p, m: xn3_emit(p, 1, m[1]),
                 halves=(1,))

        with ExitStack() as sB:
            att_my = sB.enter_context(nc.sbuf_tensor("att_my", [128, 2 * S], BF16))
            K_my = sB.enter_context(nc.sbuf_tensor("K_my", [128, S], BF16))
            V_my = sB.enter_context(nc.sbuf_tensor("V_my", [128, S], BF16))
            Q_my = xnf


            HW = min(256, BLK)
            NHF = BLK // HW
            with ExitStack() as sQ:
                wqf = sQ.enter_context(nc.sbuf_tensor("wqf", [128, DC * 128], F32))
                wqb = sQ.enter_context(nc.sbuf_tensor("wqb", [128, DC * 512], BF16))
                xge = sQ.enter_context(nc.sbuf_tensor("xge", [128, 2 * DC * HW], BF16))
                vsc = sQ.enter_context(nc.sbuf_tensor("vsc", [128, HW], BF16))
                psB = [sQ.enter_context(nc.psum_tensor(_nm("ps"), [128, HW], F32))
                       for _ in range(4)]
                psT = sQ.enter_context(nc.psum_tensor(_nm("ps"), [128, 512], F32))

                p = Phase(nc)

                v_wqb = 0
                gvals = {}
                for oc in range(4):
                    def fwq(e, oc=oc, _vp=v_wqb):
                        if "qb" not in gvals:
                            pid = nc.partition_id()
                            gvals["qb"] = e.compute_val((pid // 2) * 768)
                            gvals["qp"] = e.compute_val(
                                gvals["qb"] + (pid % 2) * 256)
                        if oc < 2:
                            off = e.compute_val(gvals["qp"] + oc * 128)
                        else:
                            off = e.compute_val(gvals["qb"] + 512 + (oc - 2) * 128)
                        wge(e, SE, _vp)
                        e.dma_start(
                            out=wqf[:].rearrange("p (c k) -> p c k", c=DC),
                            in_=bass.AP(wqt, off,
                                        [[3072, 128], [128 * 3072, DC], [1, 128]]),
                        ).then_inc(RG[2].h, 16)
                    p.g(fwq)
                    RG[2].reserve(16)
                    v_wq = RG[2].n

                    def fwqc(e, oc=oc, _v=v_wq):
                        e.wait_ge(RG[2].h, _v)
                        e.drain()
                        e.activation(wqb[:, oc * DC * 128:(oc + 1) * DC * 128],
                                     wqf[:], AF.Copy).then_inc(SE.h, 1)
                    p.s(fwqc)
                    v_wqb = SE.reserve()

                xld_v = {}
                bmm_v = {}
                bev_v = {}
                tr_v = 0
                vev_prev = [0]
                for b in range(16):
                    for hf in range(NHF):
                        islot = (b * NHF + hf) % 2
                        rnk = b if b < 8 else 15 - b
                        loc = (0 if b < 8 else BLK) + 0
                        colo = loc + hf * HW

                        hh = 0 if b < 8 else 1
                        colo2 = colo - hh * (TOK // 2)

                        def mk_fxl(qlo, qhi, sem):
                            def fxl(e, rnk=rnk, colo2=colo2, islot=islot, hh=hh,
                                    qlo=qlo, qhi=qhi, sem=sem,
                                    _vm=bmm_v.get(islot, 0), _vg=cc_xg[hh]):
                                e.wait_ge(CC.h, _vg)
                                wge(e, PE, _vm)
                                for qq in range(qlo, qhi):
                                    e.dma_start(
                                        out=xge[:, islot * DC * HW + qq * 4 * HW:
                                                islot * DC * HW + (qq + 1) * 4 * HW]
                                        .rearrange("p (c t) -> p c t", c=4),
                                        in_=bass.AP(
                                            xg_o[hh],
                                            (rnk * D + qq * 4 * 128) * (TOK // 2)
                                            + colo2,
                                            [[TOK // 2, 128],
                                             [128 * (TOK // 2), 4], [1, HW]]),
                                    ).then_inc(sem.h, 16)
                            return fxl
                        p.y(mk_fxl(0, 2, RS[islot]))
                        RS[islot].reserve(32)
                        p.g(mk_fxl(2, 4, RG[islot]))
                        RG[islot].reserve(32)
                        v_xl = (RS[islot], RS[islot].n, RG[islot], RG[islot].n)

                        par = 0
                        pb0 = 0

                        def fbm(e, islot=islot, pb0=pb0, _vx=v_xl, _vw=v_wqb,
                                _ve=bev_v.get(par, 0)):
                            e.wait_ge(_vx[0].h, _vx[1])
                            e.wait_ge(_vx[2].h, _vx[3])
                            e.wait_ge(SE.h, _vw)
                            wge(e, SE, _ve)
                            ins = None
                            for oc in range(4):
                                for dc in range(DC):
                                    ins = e.matmul(
                                        psB[pb0 + oc][:, 0:HW],
                                        wqb[:, oc * DC * 128 + dc * 128:
                                            oc * DC * 128 + (dc + 1) * 128],
                                        xge[:, islot * DC * HW + dc * HW:
                                            islot * DC * HW + (dc + 1) * HW],
                                        start=(dc == 0), stop=(dc == DC - 1))
                            ins.then_inc(PE.h, 1)
                        p.t(fbm)
                        bmm_v[islot] = PE.reserve()

                        gcol = b * BLK + hf * HW

                        def fbe(e, pb0=pb0, gcol=gcol, _vm=bmm_v[islot], _vt=tr_v):
                            e.wait_ge(PE.h, _vm)
                            wge(e, PE, _vt)
                            e.drain()
                            e.activation(Q_my[:, 0 * S + gcol:0 * S + gcol + HW],
                                         psB[pb0 + 0][:, 0:HW], AF.Copy)
                            e.activation(Q_my[:, 1 * S + gcol:1 * S + gcol + HW],
                                         psB[pb0 + 1][:, 0:HW], AF.Copy)
                            e.activation(K_my[:, gcol:gcol + HW],
                                         psB[pb0 + 2][:, 0:HW], AF.Copy)
                            e.activation(vsc[:], psB[pb0 + 3][:, 0:HW], AF.Copy)
                            e.drain()
                            e.nop().then_inc(SE.h, 1)
                        p.s(fbe)
                        bev_v[par] = SE.reserve()

                        def fbt(e, gcol=gcol, _ve=bev_v[par], _vp=vev_prev[0]):
                            e.wait_ge(SE.h, _ve)
                            wge(e, VE, _vp)
                            pb = psT.ap().bitcast(BF16)
                            ins = None
                            for k in range(HW // 128):
                                ins = e.transpose(pb[:, k * 128:(k + 1) * 128],
                                                  vsc[:, k * 128:(k + 1) * 128],
                                                  ident[:])
                            ins.then_inc(PE.h, 1)
                        p.t(fbt)
                        tr_v = PE.reserve()

                        def fbv(e, gcol=gcol, _vt=tr_v):
                            e.wait_ge(PE.h, _vt)
                            pb = psT.ap().bitcast(BF16)
                            e.drain()
                            e.tensor_copy(V_my[:, gcol:gcol + HW], pb[:, 0:HW])
                            e.drain()
                            e.nop().then_inc(VE.h, 1)
                        p.v(fbv)
                        vev_prev[0] = VE.reserve()
                p.run()
                barrier()

            with ExitStack() as sT:
                pt_sb = sT.enter_context(nc.sbuf_tensor("pt_sb", [128, 3 * 2 * QC], BF16))
                mask_sb = sT.enter_context(
                    nc.sbuf_tensor("mask_sb", [128, NPAT * 2 * QC], BF16))
                recs = sT.enter_context(nc.sbuf_tensor("recs", [1, 2 * QC], F32))
                recb = sT.enter_context(nc.sbuf_tensor("recb", [128, 2 * QC], BF16))
                psOU = [sT.enter_context(nc.psum_tensor(_nm("ps"), [128, 2 * QC], F32))
                        for _ in range(2)]
                psSG = [sT.enter_context(nc.psum_tensor(_nm("ps"), [1, 2 * QC], F32))
                        for _ in range(2)]
                psST = [sT.enter_context(nc.psum_tensor(_nm("ps"), [128, 2 * QC], F32))
                        for _ in range(3)]
                psRC = sT.enter_context(nc.psum_tensor(_nm("ps"), [128, 2 * QC], F32))

                p = Phase(nc)

                def fmload(e):
                    e.dma_start(out=mask_sb[:], in_=mask_d.ap()).then_inc(DX.h, 16)
                p.y(fmload)
                DX.reserve(16)
                v_mask = DX.n
                exp_v = {}
                cons_v = {}
                mask_v = {}
                v_att_ev = 0
                ktg = 0
                pend = None

                def emit_st(kt, qc, ktg):
                    qpos = qc * QC

                    def f(e, kt=kt, qpos=qpos, ktg=ktg,
                          _vex=exp_v.get(ktg % 3, 0)):
                        wge(e, SE, _vex)
                        e.matmul(
                            psST[ktg % 3][:, 0:2 * QC],
                            K_my[:, kt * 128:(kt + 1) * 128],
                            bass.AP(Q_my, qpos, [[2 * S, 128], [S, 2], [1, QC]]),
                            start=True, stop=True).then_inc(PE.h, 1)
                    p.t(f)
                    return PE.reserve()

                def emit_exp(ktg, vst, diag_ri):
                    def f(e, ktg=ktg, _vst=vst, _vcons=cons_v.get(ktg % 3, 0)):
                        e.wait_ge(PE.h, _vst)
                        wge(e, PE, _vcons)
                        e.drain()
                        e.activation(pt_sb[:, (ktg % 3) * 2 * QC:(ktg % 3 + 1) * 2 * QC],
                                     psST[ktg % 3][:, 0:2 * QC], AF.Exp,
                                     bias=expb[:, :], scale=INV_SQRT_HD).then_inc(SE.h, 1)
                    p.s(f)
                    exp_v[ktg % 3] = SE.reserve()
                    if diag_ri is not None:
                        def fm(e, ktg=ktg, ri=diag_ri, _ve=exp_v[ktg % 3],
                               _vm=v_mask):
                            e.wait_ge(SE.h, _ve)
                            e.wait_ge(DX.h, _vm)
                            e.drain()
                            e.tensor_tensor(
                                pt_sb[:, (ktg % 3) * 2 * QC:(ktg % 3 + 1) * 2 * QC],
                                pt_sb[:, (ktg % 3) * 2 * QC:(ktg % 3 + 1) * 2 * QC],
                                mask_sb[:, ri * 2 * QC:(ri + 1) * 2 * QC], ALU.mult)
                            e.drain()
                            e.nop().then_inc(VE.h, 1)
                        p.v(fm)
                        mask_v[ktg] = VE.reserve()

                def emit_sigpv(kt, qc, nkt, ktg):
                    bk = qc % 2

                    def f(e, kt=kt, ktg=ktg, bk=bk, first=(kt == 0),
                          last=(kt == nkt - 1),
                          _vex=exp_v[ktg % 3], _vm=mask_v.get(ktg, 0),
                          _vae=att_ev[bk] if kt == 0 else 0):
                        e.wait_ge(SE.h, _vex)
                        wge(e, VE, _vm)
                        if first:
                            wge(e, VE, _vae)
                        pt = pt_sb[:, (ktg % 3) * 2 * QC:(ktg % 3 + 1) * 2 * QC]
                        e.matmul(psSG[bk][:, 0:2 * QC], ones_c[:], pt,
                                 start=first, stop=last)
                        e.matmul(psOU[bk][:, 0:2 * QC],
                                 V_my[:, kt * 128:(kt + 1) * 128],
                                 pt, start=first, stop=last).then_inc(PE.h, 1)
                    p.t(f)
                    cons_v[ktg % 3] = PE.reserve()
                    return cons_v[ktg % 3]

                att_ev = {0: 0, 1: 0}
                pending_norm = [None]

                def emit_norm():
                    if pending_norm[0] is None:
                        return
                    qc_, qpos_, v_last_ = pending_norm[0]
                    pending_norm[0] = None
                    bk = qc_ % 2

                    def fr1(e, _vl=v_last_, bk=bk):
                        e.wait_ge(PE.h, _vl)
                        e.drain()
                        e.reciprocal(recs[:], psSG[bk][:, 0:2 * QC])
                        e.drain()
                        e.nop().then_inc(VE.h, 1)
                    p.v(fr1)
                    v_rec = VE.reserve()

                    def fr2(e, _v=v_rec):
                        e.wait_ge(VE.h, _v)
                        e.matmul(psRC[:, 0:2 * QC], ones_r[:], recs[:],
                                 start=True, stop=True).then_inc(PE.h, 1)
                    p.t(fr2)
                    v_rcb = PE.reserve()

                    def fr3(e, qpos=qpos_, bk=bk, _v=v_rcb):
                        e.wait_ge(PE.h, _v)
                        e.drain()
                        e.tensor_copy(recb[:], psRC[:, 0:2 * QC])
                        e.drain()
                        for hp in range(2):
                            e.tensor_tensor(
                                att_my[:, hp * S + qpos:hp * S + qpos + QC],
                                psOU[bk][:, hp * QC:(hp + 1) * QC],
                                recb[:, hp * QC:(hp + 1) * QC], ALU.mult)
                        e.drain()
                        e.nop().then_inc(VE.h, 1)
                    p.v(fr3)
                    att_ev[bk] = VE.reserve()

                    if (qc_ + 1) % (NQCG // 4) == 0 and qc_ != NQCG - 1:
                        ch = (qc_ + 1) // (NQCG // 4) - 1
                        S4 = S // 4

                        def fg1a(e, ch=ch, S4=S4, _v=att_ev[bk]):
                            e.wait_ge(VE.h, _v)
                            for hp in range(2):
                                e.dma_start(
                                    out=at_i[ch].ap()[hp * 128:(hp + 1) * 128, :],
                                    in_=att_my[:, hp * S + ch * S4:
                                               hp * S + (ch + 1) * S4],
                                ).then_inc(DG.h, 16)
                        p.g(fg1a)
                        DG.reserve(32)
                        vga = DG.n

                        def fg2a(e, ch=ch, _v=vga):
                            e.wait_ge(DG.h, _v)
                            e.collective_compute(
                                "AllGather", ALU.bypass,
                                replica_groups=[list(range(NCORE))],
                                ins=[at_i[ch].ap()],
                                outs=[at_og.ap()[ch * NCORE * 256:
                                                 (ch + 1) * NCORE * 256, :]],
                            ).then_inc(CC.h, 1)
                        p.g(fg2a)
                        CC.reserve(1)
                        cc_at[ch] = CC.n

                pend_q = []
                for qc in range(NQCG):
                    qpos = qc * QC
                    nkt = (qpos + QC) // 128
                    d0 = qpos // 128
                    for kt in range(nkt):
                        vst = emit_st(kt, qc, ktg)
                        if len(pend_q) >= 2:
                            emit_sigpv(*pend_q.pop(0))
                        emit_exp(ktg, vst, kt - d0 if kt >= d0 else None)
                        pend_q.append((kt, qc, nkt, ktg))
                        ktg += 1
                        if kt == 1:
                            emit_norm()
                    while pend_q:
                        v_last = emit_sigpv(*pend_q.pop(0))
                    pending_norm[0] = (qc, qpos, v_last)
                emit_norm()
                p.run()
                barrier()

            p = Phase(nc)

            S4 = S // 4

            def fg1(e, S4=S4):
                for hp in range(2):
                    e.dma_start(out=at_i[3].ap()[hp * 128:(hp + 1) * 128, :],
                                in_=att_my[:, hp * S + 3 * S4:(hp + 1) * S]
                                ).then_inc(DG.h, 16)
            p.g(fg1)
            DG.reserve(32)
            vg1 = DG.n

            def fg2(e, _v=vg1):
                e.wait_ge(DG.h, _v)
                e.collective_compute(
                    "AllGather", ALU.bypass, replica_groups=[list(range(NCORE))],
                    ins=[at_i[3].ap()],
                    outs=[at_og.ap()[3 * NCORE * 256:4 * NCORE * 256, :]],
                ).then_inc(CC.h, 1)
            p.g(fg2)
            CC.reserve(1)
            cc_at[3] = CC.n

            CHK = NCORE * 256 * S4

            g3v = {}

            def fg3a(e, S4=S4, _v=cc_at[3]):
                e.wait_ge(CC.h, _v)
                pid = nc.partition_id()
                of0 = e.compute_val((pid // 4) * CHK + (pid % 4) * BLK)
                g3v["of0"] = of0
                e.dma_start(
                    out=bass.AP(Q_my, 0,
                                [[DC * TOK, 128], [8 * TOK, 2],
                                 [TOK, 8], [1, BLK]]),
                    in_=bass.AP(at_og, of0,
                                [[S4, 128], [8 * 128 * S4, 2],
                                 [128 * S4, 8], [1, BLK]]),
                ).then_inc(DG.h, 16)
            p.g(fg3a)
            DG.reserve(16)

            def fg3b(e, S4=S4):
                of1 = e.compute_val(3 * CHK + (S4 - BLK) - g3v["of0"])
                e.dma_start(
                    out=bass.AP(Q_my, BLK,
                                [[DC * TOK, 128], [8 * TOK, 2],
                                 [TOK, 8], [1, BLK]]),
                    in_=bass.AP(at_og, of1,
                                [[S4, 128], [8 * 128 * S4, 2],
                                 [128 * S4, 8], [1, BLK]]),
                ).then_inc(DG.h, 16)
            p.g(fg3b)
            DG.reserve(16)
            v_att_in = DG.n

            def fg4(e, _v=v_att_in):
                e.wait_ge(DG.h, _v)
            p.g(fg4)
            p.run()
            barrier()

            with ExitStack() as sP:
                wtf = sP.enter_context(nc.sbuf_tensor("wpf", [128, 4 * 512], F32))
                wtb = sP.enter_context(nc.sbuf_tensor("wpb", [128, 4 * 512], BF16))
                psO = [sP.enter_context(nc.psum_tensor(_nm("ps"), [128, 512], F32))
                       for _ in range(NT)]
                p = Phase(nc)
                t_cast = {}
                t_mm = {}
                v_evac = 0
                for et in range(4):
                    for ec in range(DC):
                        slot = (et * DC + ec) % 4

                        def fd(e, et=et, ec=ec, slot=slot, _vc=t_cast.get(slot, 0)):
                            wge(e, SE, _vc)
                            e.dma_start(
                                out=wtf[:, slot * 512:(slot + 1) * 512],
                                in_=wpt.ap()[ec * 128:(ec + 1) * 128,
                                             et * 512:(et + 1) * 512],
                            ).then_inc(RS[slot].h, 16)
                        p.y(fd)
                        RS[slot].reserve(16)
                        vd = RS[slot].n

                        def fc(e, slot=slot, _vd=vd, _vm=t_mm.get(slot, 0)):
                            e.wait_ge(RS[slot].h, _vd)
                            wge(e, PE, _vm)
                            e.drain()
                            e.activation(wtb[:, slot * 512:(slot + 1) * 512],
                                         wtf[:, slot * 512:(slot + 1) * 512],
                                         AF.Copy).then_inc(SE.h, 1)
                        p.s(fc)
                        t_cast[slot] = SE.reserve()

                        def fmm(e, et=et, ec=ec, slot=slot, _vc=t_cast[slot],
                                _ve=v_evac):
                            e.wait_ge(SE.h, _vc)
                            if ec == 0:
                                wge(e, VE, _ve)
                            ins = None
                            for tc in range(NT):
                                ins = e.matmul(
                                    psO[tc][:, 0:512],
                                    Q_my[:, ec * TOK + tc * 128:ec * TOK + (tc + 1) * 128],
                                    wtb[:, slot * 512:(slot + 1) * 512],
                                    start=(ec == 0), stop=(ec == DC - 1))
                            ins.then_inc(PE.h, 1)
                        p.t(fmm)
                        t_mm[slot] = PE.reserve()
                    v_pj = t_mm[slot]

                    def fev(e, et=et, _vm=v_pj):
                        e.wait_ge(PE.h, _vm)
                        e.drain()
                        for tc in range(NT):
                            e.tensor_tensor(
                                x_sb[:, tc * D + et * 512:tc * D + (et + 1) * 512],
                                x_sb[:, tc * D + et * 512:tc * D + (et + 1) * 512],
                                psO[tc][:, 0:512], ALU.add)
                            e.drain()
                        e.nop().then_inc(VE.h, 1)
                    p.v(fev)
                    v_evac = VE.reserve()
                p.run()
            barrier()

        if stage <= 4:
            emit_output()
            return nc
        ln_phase()
        ffn_pass()

        emit_output()
    return nc


def _rope_perm():
    return np.concatenate([np.arange(0, 1024, 2), np.arange(1, 1024, 2),
                           np.arange(1024, 2048)])


def _ic_perm():
    perm = np.empty(I_FFN, dtype=np.int64)
    for m in range(MQ):
        for j in range(4):
            for pp in range(128):
                perm[(4 * m + j) * 128 + pp] = 512 * m + 4 * pp + j
    return perm


def prep_inputs(kw, BLK=512, QC=256):
    S = 16 * BLK
    TOK = 2 * BLK
    NPAT = QC // 128
    pi = _rope_perm()
    pI = _ic_perm()

    X = np.asarray(kw["L0_attn_a_X"], dtype=np.float32)[0]
    W_in = np.asarray(kw["L0_attn_a_W_in"], dtype=np.float32)
    b_in = np.asarray(kw["L0_attn_a_b_in"], dtype=np.float32)
    W_out = np.asarray(kw["L0_attn_a_W_out"], dtype=np.float32)
    b_out = np.asarray(kw["L0_attn_a_b_out"], dtype=np.float32)
    Wqkv = np.asarray(kw["L0_attn_b_weight_qkv"], dtype=np.float32)
    Wproj = np.asarray(kw["L0_attn_b_weight_proj"], dtype=np.float32)
    wg = np.asarray(kw["L0_ffn_w_gate"], dtype=np.float32)
    wu = np.asarray(kw["L0_ffn_w_up"], dtype=np.float32)
    wd = np.asarray(kw["L0_ffn_w_down"], dtype=np.float32)

    w1 = W_in.reshape(3, D, D)[:, pi, :].reshape(3 * D, D)
    w1t = np.ascontiguousarray(w1.T)
    b1p = b_in.reshape(3, D)[:, pi].reshape(3 * D)
    b1c = np.ascontiguousarray(b1p.reshape(3 * DC, 128).T)
    w2t = np.ascontiguousarray(W_out[:, pi].T)
    b2r = np.ascontiguousarray(b_out.reshape(1, D))
    wqt = np.ascontiguousarray(Wqkv.T)
    wpt = np.ascontiguousarray(Wproj.T)
    wgp = np.ascontiguousarray(wg[:, pI])
    wup = np.ascontiguousarray(wu[:, pI])
    wdp = np.ascontiguousarray(wd[pI, :])

    def tile_gu(w):
        out = np.empty((32 * 128, 4096), np.float32)
        for b in range(32):
            m_, jp_ = b // 2, b % 2
            blkc = w[:, (4 * m_ + 2 * jp_) * 128:(4 * m_ + 2 * jp_ + 2) * 128]
            out[b * 128:(b + 1) * 128] = (
                blkc.reshape(16, 128, 256).transpose(1, 0, 2).reshape(128, 4096))
        return np.ascontiguousarray(out)

    def tile_wd(w):
        out = np.empty((64 * 128, 2048), np.float32)
        for b in range(64):
            Ht_, g4_ = b // 16, b % 16
            blkc = w[g4_ * 512:(g4_ + 1) * 512, Ht_ * 512:(Ht_ + 1) * 512]
            out[b * 128:(b + 1) * 128] = (
                blkc.reshape(4, 128, 512).transpose(1, 0, 2).reshape(128, 2048))
        return np.ascontiguousarray(out)

    wgp = tile_gu(wgp)
    wup = tile_gu(wup)
    wdp = tile_wd(wdp)

    ident = np.eye(128, dtype=BF)

    mask = np.zeros((128, NPAT * 2 * QC), dtype=BF)
    for ri in range(NPAT):
        kk = np.arange(128)[:, None] + ri * 128
        qq = np.arange(QC)[None, :]
        m = (kk <= qq).astype(np.float32).astype(BF)
        for hp in range(2):
            mask[:, ri * 2 * QC + hp * QC:(ri * 2) * QC + (hp + 1) * QC] = m

    inv = (10000.0 ** (-np.arange(512, dtype=np.float64) * 2.0 / 1024.0))

    in_maps = []
    shard_rows = []
    for c in range(NCORE):
        blo = np.arange(c * BLK, (c + 1) * BLK)
        bhi = np.arange(S - (c + 1) * BLK, S - c * BLK)
        rows = np.concatenate([blo, bhi])
        shard_rows.append(rows)
        pos = rows.astype(np.float64)
        ang = pos[None, :] * inv[:, None]
        cosd = np.cos(ang).astype(np.float32).astype(BF)
        sind = np.sin(ang).astype(np.float32).astype(BF)
        im = {
            "x_in": np.ascontiguousarray(X[rows]),
            "w1t": w1t, "b1c": b1c, "w2t": w2t, "b2r": b2r,
            "wqt": wqt, "wpt": wpt,
            "wg_d": wgp, "wu_d": wup, "wd_d": wdp,
            "cos_d": cosd, "sin_d": sind,
            "mask_d": mask, "ident_d": ident,
        }
        in_maps.append(im)
    return in_maps, shard_rows


def kernel(**kw):
    BLK, QC = 512, 256
    S = 16 * BLK
    in_maps, shard_rows = prep_inputs(kw, BLK, QC)
    nc = build_program(BLK, QC)
    res = run_bass_kernel_spmd(nc, in_maps, core_ids=list(range(NCORE)))
    out = np.zeros((1, S, D), dtype=np.float32)
    for c in range(NCORE):
        out[0, shard_rows[c]] = res.results[c]["out"]
    return out


# version 16 (final)
# speedup vs baseline: 1.0169x; 1.0169x over previous
import numpy as np
import ml_dtypes
from contextlib import ExitStack

import concourse.bass as bass
import concourse.mybir as mybir
from concourse.bass_utils import run_bass_kernel_spmd

F32 = mybir.dt.float32
BF16 = mybir.dt.bfloat16
AF = mybir.ActivationFunctionType
ALU = mybir.AluOpType
AX = mybir.AxisListType.X

NCORE = 8
D = 2048
DC = 16
I_FFN = 8192
MQ = 16
EXP_BIAS = -20.0
INV_SQRT_HD = float(1.0 / np.sqrt(128.0))
BF = ml_dtypes.bfloat16


_uid = [0]


def _nm(pfx):
    _uid[0] += 1
    return f"{pfx}{_uid[0]}"


class Ctr:
    def __init__(self, handle):
        self.h = handle
        self.n = 0

    def reserve(self, k=1):
        self.n += k
        return self.n


class Phase:
    def __init__(self, nc):
        self.nc = nc
        self.ops = {k: [] for k in ("sync", "tensor", "vector", "scalar", "gpsimd")}

    def y(self, fn):
        self.ops["sync"].append(fn)

    def t(self, fn):
        self.ops["tensor"].append(fn)

    def v(self, fn):
        self.ops["vector"].append(fn)

    def s(self, fn):
        self.ops["scalar"].append(fn)

    def g(self, fn):
        self.ops["gpsimd"].append(fn)

    def run(self):
        with self.nc.Block() as blk:
            for name, deco in (("sync", blk.sync), ("tensor", blk.tensor),
                               ("vector", blk.vector), ("scalar", blk.scalar),
                               ("gpsimd", blk.gpsimd)):
                fns = self.ops[name]
                if fns:
                    def body(e, _fns=tuple(fns)):
                        for f in _fns:
                            f(e)
                    deco(body)


def build_program(BLK=512, QC=256, stage=99):
    S = 16 * BLK
    TOK = 2 * BLK
    NT = TOK // 128
    TH = TOK // 2
    NTH = TH // 128
    NPAT = QC // 128
    NQCG = S // QC
    TSPL = [(o, min(512, TOK - o)) for o in range(0, TOK, 512)]

    nc = bass.Bass("TRN2", target_bir_lowering=False, debug=False)

    x_in = nc.dram_tensor("x_in", [TOK, D], F32, kind="ExternalInput")
    w1t = nc.dram_tensor("w1t", [D, 3 * D], F32, kind="ExternalInput")
    b1c = nc.dram_tensor("b1c", [128, 3 * DC], F32, kind="ExternalInput")
    w2t = nc.dram_tensor("w2t", [D, D], F32, kind="ExternalInput")
    b2r = nc.dram_tensor("b2r", [1, D], F32, kind="ExternalInput")
    wqt = nc.dram_tensor("wqt", [D, 3072], F32, kind="ExternalInput")
    wpt = nc.dram_tensor("wpt", [D, D], F32, kind="ExternalInput")
    wg_d = nc.dram_tensor("wg_d", [32 * 128, 4096], F32, kind="ExternalInput")
    wu_d = nc.dram_tensor("wu_d", [32 * 128, 4096], F32, kind="ExternalInput")
    wd_d = nc.dram_tensor("wd_d", [64 * 128, 2048], F32, kind="ExternalInput")
    cos_d = nc.dram_tensor("cos_d", [512, TOK], BF16, kind="ExternalInput")
    sin_d = nc.dram_tensor("sin_d", [512, TOK], BF16, kind="ExternalInput")
    mask_d = nc.dram_tensor("mask_d", [128, NPAT * 2 * QC], BF16, kind="ExternalInput")
    ident_d = nc.dram_tensor("ident_d", [128, 128], BF16, kind="ExternalInput")
    out_d = nc.dram_tensor("out", [TOK, D], F32, kind="ExternalOutput")

    wgb_d = nc.dram_tensor("wgb_d", [32 * 128, 4096], BF16)
    wub_d = nc.dram_tensor("wub_d", [32 * 128, 4096], BF16)
    wdb_d = nc.dram_tensor("wdb_d", [64 * 128, 2048], BF16)
    ar_i = nc.dram_tensor("ar_i", [128, DC], F32)
    ar_o = nc.dram_tensor("ar_o", [128, DC], F32)
    xg_i = [nc.dram_tensor(f"xg_i{h}", [D, TOK // 2], BF16) for h in range(2)]
    xg_o = [nc.dram_tensor(f"xg_o{h}", [NCORE * D, TOK // 2], BF16)
            for h in range(2)]
    at_i = [nc.dram_tensor(f"at_i{h}", [256, S // 4], BF16) for h in range(4)]
    at_og = nc.dram_tensor("at_og", [4 * NCORE * 256, S // 4], BF16)

    es = ExitStack()
    with es:
        x_sb = es.enter_context(nc.sbuf_tensor("x_sb", [128, NT * D], F32))
        xnf = es.enter_context(nc.sbuf_tensor("xnf", [128, DC * TOK], BF16))
        b1_sb = es.enter_context(nc.sbuf_tensor("b1_sb", [128, 3 * DC], F32))
        ident = es.enter_context(nc.sbuf_tensor("ident", [128, 128], BF16))
        ones_c = es.enter_context(nc.sbuf_tensor("ones_c", [128, 1], BF16))
        ones_r = es.enter_context(nc.sbuf_tensor("ones_r", [1, 128], F32))
        expb = es.enter_context(nc.sbuf_tensor("expb", [128, 1], F32))
        stat = es.enter_context(nc.sbuf_tensor("stat", [128, 8 * NT], F32))
        A_sb = es.enter_context(nc.sbuf_tensor("A_sb", [128, DC], F32))
        sCV = es.enter_context(ExitStack())
        cf32 = sCV.enter_context(nc.sbuf_tensor("cf32", [128, 2 * 1024], F32))

        PE = Ctr(es.enter_context(nc.semaphore("sPE")))
        VE = Ctr(es.enter_context(nc.semaphore("sVE")))
        SE = Ctr(es.enter_context(nc.semaphore("sSE")))
        GE = Ctr(es.enter_context(nc.semaphore("sGE")))
        DW = Ctr(es.enter_context(nc.semaphore("sDW")))
        RS = [Ctr(es.enter_context(nc.semaphore(f"sRS{i}"))) for i in range(11)]
        RG = [Ctr(es.enter_context(nc.semaphore(f"sRG{i}"))) for i in range(3)]
        CVW = [Ctr(es.enter_context(nc.semaphore(f"sCVW{i}"))) for i in range(8)]
        RP = [Ctr(es.enter_context(nc.semaphore(f"sRP{i}"))) for i in range(3)]
        DX = Ctr(es.enter_context(nc.semaphore("sDX")))
        DG = Ctr(es.enter_context(nc.semaphore("sDG")))
        CC = Ctr(es.enter_context(nc.semaphore("sCC")))
        PH = Ctr(es.enter_context(nc.semaphore("sPH")))

        def wge(e, ctr, v):
            if v > 0:
                e.wait_ge(ctr.h, v)

        def barrier():
            ph = Phase(nc)
            tgt = PH.n + 5
            PH.n = tgt

            def mk():
                def f(e):
                    e.drain()
                    e.nop().then_inc(PH.h, 1)
                    e.wait_ge(PH.h, tgt)
                return f
            for name in ph.ops:
                ph.ops[name].append(mk())
            ph.run()

        def emit_output():
            p = Phase(nc)

            def fout(e):
                for tc in range(NT):
                    e.dma_start(out=out_d.ap()[tc * 128:(tc + 1) * 128, :],
                                in_=x_sb[:, tc * D:(tc + 1) * D]).then_inc(DX.h, 16)
                    DX.reserve(16)
                e.wait_ge(DX.h, DX.n)
            p.y(fout)
            p.run()

        conv_items = []
        conv_done = [0]
        conv_ridx = [0]
        rp_cast = {}
        stg_wb = {}
        fresh_pc = {}
        wb_done = {}
        cbuf = {}

        def conv_add_one(kind, b, stg, base, npc, src_fn, cache_d):
            cast_waits = {}
            for pc in range(npc):
                ridx = conv_ridx[0]
                conv_ridx[0] += 1
                sl = ridx % 2
                eng_i = 0

                def item(p, b=b, pc=pc, sl=sl, stg=stg, base=base,
                         eng_i=eng_i, kind=kind, npc=npc,
                         src_fn=src_fn, cache_d=cache_d, cast_waits=cast_waits):
                    _vc = rp_cast.get(sl)

                    def fl(e, b=b, pc=pc, sl=sl, _vc=_vc):
                        if _vc is not None:
                            e.wait_ge(_vc[0].h, _vc[1])
                        e.dma_start(out=cf32[:, sl * 1024:(sl + 1) * 1024],
                                    in_=src_fn(b, pc)).then_inc(RP[sl].h, 16)
                    p.y(fl)
                    RP[sl].reserve(16)
                    vl = RP[sl].n

                    ectr = (SE, VE)[eng_i]
                    ring = cbuf["ring"]

                    def fc(e, pc=pc, sl=sl, base=base, _vl=vl,
                           _wb=stg_wb.get(stg, ()),
                           _pe=cbuf["guard_fn"](stg, pc), ring=ring,
                           eng_i=eng_i):
                        e.wait_ge(RP[sl].h, _vl)
                        for ctr, val in _wb:
                            e.wait_ge(ctr.h, val)
                        wge(e, PE, _pe)
                        dst = ring[:, base + pc * 1024:base + (pc + 1) * 1024]
                        srcv = cf32[:, sl * 1024:(sl + 1) * 1024]
                        if eng_i == 0:
                            e.activation(dst, srcv, AF.Copy).then_inc(SE.h, 1)
                        else:
                            e.tensor_copy(dst, srcv).then_inc(VE.h, 1)
                    (p.s, p.v)[eng_i](fc)
                    v = ectr.reserve()
                    rp_cast[sl] = (ectr, v)
                    cast_waits[ectr] = v
                    fresh_pc[(kind, b, pc)] = (ectr, v)

                    if pc == npc - 1:
                        cvw = CVW[stg if stg < 4 else stg - 6]
                        cw = tuple(cast_waits.items())

                        def fw(e, _cw=cw, b=b, base=base, npc=npc,
                               cache_d=cache_d, ring=ring, cvw=cvw):
                            for ctr, val in _cw:
                                e.wait_ge(ctr.h, val)
                            e.dma_start(
                                out=cache_d.ap()[b * 128:(b + 1) * 128, :],
                                in_=ring[:, base:base + npc * 1024],
                            ).then_inc(cvw.h, 16)
                        p.g(fw)
                        cvw.reserve(16)
                        stg_wb[stg] = ((cvw, cvw.n),)
                        wb_done[(kind, b)] = (cvw, cvw.n)
                conv_items.append(item)

        def conv_pump(p, n, cap=None):
            k = 0
            while conv_items and k < n:
                if cap is not None and conv_done[0] >= cap:
                    return
                conv_items.pop(0)(p)
                conv_done[0] += 1
                k += 1

        GU_ITEMS = 256

        def queue_conv():
            for b in range(32):
                for wi, (src_d, kind, cache) in enumerate(
                        ((wg_d, "g", wgb_d), (wu_d, "u", wub_d))):
                    widx = b * 2 + wi

                    def sf(bb, pc, src_d=src_d):
                        return src_d.ap()[bb * 128:(bb + 1) * 128,
                                          pc * 1024:(pc + 1) * 1024]
                    conv_add_one(kind, b, widx % 3, (widx % 3) * 4096, 4,
                                 sf, cache)

            def sfd(bb, pc):
                return wd_d.ap()[bb * 128:(bb + 1) * 128,
                                 pc * 1024:(pc + 1) * 1024]
            for b in range(64):
                conv_add_one("d", b, 10 + b % 3, (b % 3) * 2048, 2, sfd, wdb_d)

        def emit_ln(p, psD, xn_tok, sq_scr):
            marks = [0, 0]
            tr_half = [0, 0]
            ev_half = [0, 0]
            NH2 = NT // 2
            for hlf in range(2):
                lo, hi = hlf * NH2, (hlf + 1) * NH2
                for tc in range(lo, hi):
                    def f(e, tc=tc):
                        e.drain()
                        e.activation(sq_scr[:], x_sb[:, tc * D:(tc + 1) * D],
                                     AF.Square,
                                     accum_out=stat[:, NT + tc:NT + tc + 1])
                        e.drain()
                        e.activation(sq_scr[:], x_sb[:, tc * D:(tc + 1) * D],
                                     AF.Identity,
                                     accum_out=stat[:, tc:tc + 1]).then_inc(SE.h, 1)
                    p.s(f)
                    SE.reserve()
                v_stats = SE.n

                def fc(e, lo=lo, hi=hi, _vs=v_stats):
                    e.wait_ge(SE.h, _vs)
                    e.drain()
                    e.tensor_scalar(stat[:, 2 * NT + lo:2 * NT + hi],
                                    stat[:, lo:hi], 1.0 / D, None, ALU.mult)
                    e.drain()
                    e.tensor_tensor(stat[:, lo:hi],
                                    stat[:, 2 * NT + lo:2 * NT + hi],
                                    stat[:, 2 * NT + lo:2 * NT + hi], ALU.mult)
                    e.tensor_scalar(stat[:, 3 * NT + lo:3 * NT + hi],
                                    stat[:, NT + lo:NT + hi], 1.0 / D,
                                    None, ALU.mult)
                    e.drain()
                    e.tensor_tensor(stat[:, 3 * NT + lo:3 * NT + hi],
                                    stat[:, 3 * NT + lo:3 * NT + hi],
                                    stat[:, lo:hi], ALU.subtract)
                    e.drain()
                    e.tensor_scalar(stat[:, 3 * NT + lo:3 * NT + hi],
                                    stat[:, 3 * NT + lo:3 * NT + hi],
                                    1e-5, None, ALU.add).then_inc(VE.h, 1)
                p.v(fc)
                vvar = VE.reserve()

                def fr(e, lo=lo, hi=hi, _vv=vvar):
                    e.wait_ge(VE.h, _vv)
                    e.drain()
                    e.activation(stat[:, lo:hi],
                                 stat[:, 3 * NT + lo:3 * NT + hi], AF.Ln)
                    e.drain()
                    e.activation(stat[:, 3 * NT + lo:3 * NT + hi],
                                 stat[:, lo:hi], AF.Exp,
                                 scale=-0.5).then_inc(SE.h, 1)
                p.s(fr)
                vr = SE.reserve()

                def fs(e, lo=lo, hi=hi, _vr=vr):
                    e.wait_ge(SE.h, _vr)
                    e.drain()
                    e.tensor_scalar(stat[:, 4 * NT + lo:4 * NT + hi],
                                    stat[:, 3 * NT + lo:3 * NT + hi], 1.0,
                                    None, ALU.add)
                    e.tensor_tensor(stat[:, 5 * NT + lo:5 * NT + hi],
                                    stat[:, 2 * NT + lo:2 * NT + hi],
                                    stat[:, 3 * NT + lo:3 * NT + hi], ALU.mult)
                    e.drain()
                    e.tensor_scalar(stat[:, 5 * NT + lo:5 * NT + hi],
                                    stat[:, 5 * NT + lo:5 * NT + hi], -1.0,
                                    None, ALU.mult).then_inc(VE.h, 1)
                p.v(fs)
                vs12 = VE.reserve()

                for tc in range(lo, hi):
                    half = tc % 2
                    xv = xn_tok[:, half * D:(half + 1) * D]

                    def fn(e, tc=tc, xv=xv, _vs=vs12, _pe=tr_half[half]):
                        e.wait_ge(VE.h, _vs)
                        wge(e, PE, _pe)
                        e.drain()
                        e.tensor_scalar(
                            xv, x_sb[:, tc * D:(tc + 1) * D],
                            stat[:, 2 * NT + tc:2 * NT + tc + 1],
                            stat[:, 3 * NT + tc:3 * NT + tc + 1],
                            ALU.subtract, ALU.mult).then_inc(VE.h, 1)
                    p.v(fn)
                    vn = VE.reserve()

                    pA, pB = psD[2 * half], psD[2 * half + 1]

                    def ft(e, xv=xv, pA=pA, pB=pB, _vn=vn, _ve=ev_half[half]):
                        e.wait_ge(VE.h, _vn)
                        wge(e, VE, _ve)
                        ins = None
                        for dc in range(16):
                            pb = (pA if dc < 8 else pB).ap().bitcast(BF16)
                            ins = e.transpose(pb[:, (dc % 8) * 128:(dc % 8 + 1) * 128],
                                              xv[:, dc * 128:(dc + 1) * 128], ident[:])
                        ins.then_inc(PE.h, 1)
                    p.t(ft)
                    tr_half[half] = PE.reserve()

                    def fe(e, tc=tc, pA=pA, pB=pB, _vt=tr_half[half]):
                        e.wait_ge(PE.h, _vt)
                        e.drain()
                        for b, pp in enumerate((pA, pB)):
                            pb = pp.ap().bitcast(BF16)
                            e.tensor_copy(
                                bass.AP(xnf, (b * 8) * TOK + tc * 128,
                                        [[DC * TOK, 128], [TOK, 8], [1, 128]]),
                                pb[:, 0:1024].rearrange("p (c k) -> p c k", c=8))
                        e.drain()
                        e.nop().then_inc(VE.h, 1)
                    p.v(fe)
                    ev_half[half] = VE.reserve()
                marks[hlf] = VE.n
            return marks

        ln_ctr = [0]

        def ln_phase(hook=None):
            ln_ctr[0] += 1
            with ExitStack() as sL:
                xn_tok = sL.enter_context(
                    nc.sbuf_tensor(_nm("xn_tok"), [128, 2 * D], BF16))
                sq_l = sL.enter_context(nc.sbuf_tensor(_nm("sq"), [128, D], BF16))
                psD = [sL.enter_context(nc.psum_tensor(_nm("ps"), [128, 512], F32))
                       for _ in range(4)]
                p = Phase(nc)
                marks = emit_ln(p, psD, xn_tok, sq_l)
                if hook is not None:
                    hook(p, marks)
                p.run()
                barrier()

        p = Phase(nc)
        for tc in range(NT):
            def f(e, tc=tc):
                e.dma_start(out=x_sb[:, tc * D:(tc + 1) * D],
                            in_=x_in.ap()[tc * 128:(tc + 1) * 128, :]).then_inc(DX.h, 16)
            p.y(f)
            DX.reserve(16)
        for dst, src in [(b1_sb, b1c), (ident, ident_d)]:
            def f(e, dst=dst, src=src):
                e.dma_start(out=dst[:], in_=src.ap()).then_inc(DX.h, 16)
            p.y(f)
            DX.reserve(16)
        vload = DX.n

        def fv(e, _v=vload):
            e.wait_ge(DX.h, _v)
            e.memset(ones_c[:], 1.0)
            e.memset(ones_r[:], 1.0)
            e.memset(expb[:], EXP_BIAS)
            e.nop().then_inc(VE.h, 1)
        p.v(fv)
        VE.reserve()
        p.run()
        barrier()

        ln_phase()
        if stage <= 1:
            emit_output()
            return nc

        with ExitStack() as sA:
            kq = sA.enter_context(nc.sbuf_tensor("kq", [128, DC * TOK], BF16))
            sA2 = sA.enter_context(ExitStack())
            cos_sb = sA2.enter_context(nc.sbuf_tensor("cos_sb", [128, 4 * TOK], BF16))
            sin_sb = sA2.enter_context(nc.sbuf_tensor("sin_sb", [128, 4 * TOK], BF16))
            esav = sA2.enter_context(nc.sbuf_tensor("esav", [128, TOK], BF16))
            rscr = sA2.enter_context(nc.sbuf_tensor("rscr", [128, TOK], BF16))
            vch = sA2.enter_context(nc.sbuf_tensor("vch", [128, 2 * TOK], BF16))
            vscr = sA2.enter_context(nc.sbuf_tensor("vscr", [128, TOK], BF16))
            wb_sb = sA2.enter_context(nc.sbuf_tensor("wb_sb", [128, TOK], BF16))
            nrm = sA2.enter_context(nc.sbuf_tensor("nrm", [1, TOK], F32))
            w1f = sA2.enter_context(nc.sbuf_tensor("w1f", [128, 2 * 1024], F32))
            w1b = sA2.enter_context(nc.sbuf_tensor("w1b", [128, 2 * DC * 128], BF16))
            sq_scr = sA2.enter_context(nc.sbuf_tensor("sq_a", [128, D], BF16))
            psQ = [sA2.enter_context(nc.psum_tensor(_nm("ps"), [128, 512], F32))
                   for _ in range(2 * len(TSPL))]
            psN = [sA2.enter_context(nc.psum_tensor(_nm("ps"), [1, 512], F32))
                   for _ in range(len(TSPL))]
            psW = [sA2.enter_context(nc.psum_tensor(_nm("ps"), [128, 512], F32))
                   for _ in range(len(TSPL))]

            def rope_inplace(e, slot_t):
                for q in range(4):
                    E = slot_t[:, q * TOK:(q + 1) * TOK]
                    O = slot_t[:, (4 + q) * TOK:(5 + q) * TOK]
                    Ct = cos_sb[:, q * TOK:(q + 1) * TOK]
                    St = sin_sb[:, q * TOK:(q + 1) * TOK]
                    e.drain()
                    e.tensor_copy(esav[:], E)
                    e.tensor_tensor(rscr[:], O, St, ALU.mult)
                    e.drain()
                    e.tensor_tensor(E, E, Ct, ALU.mult)
                    e.drain()
                    e.tensor_tensor(E, E, rscr[:], ALU.subtract)
                    e.tensor_tensor(O, O, Ct, ALU.mult)
                    e.drain()
                    e.tensor_tensor(rscr[:], esav[:], St, ALU.mult)
                    e.drain()
                    e.tensor_tensor(O, O, rscr[:], ALU.add)
                    e.drain()

            p = Phase(nc)

            def f_tab(e):
                e.dma_start(
                    out=cos_sb[:].rearrange("p (c t) -> p c t", c=4),
                    in_=cos_d.ap().rearrange("(c p) t -> p c t", c=4, p=128),
                ).then_inc(DX.h, 16)
                e.dma_start(
                    out=sin_sb[:].rearrange("p (c t) -> p c t", c=4),
                    in_=sin_d.ap().rearrange("(c p) t -> p c t", c=4, p=128),
                ).then_inc(DX.h, 16)
            p.y(f_tab)
            DX.reserve(32)
            v_tab = DX.n

            ec_order = list(range(16, 48)) + list(range(0, 16))
            cast_v = {}
            mm_v = {}
            evac_v = {}
            z_v = {}
            v_kro = v_qro = v_wb = v_A_done = 0

            def emit_w1(i, ec):
                slot = i % 2
                for half in range(2):
                    hslot = (i * 2 + half) % 2

                    def fd(e, ec=ec, half=half, hslot=hslot,
                           _vc=cast_v.get(("h", hslot), 0)):
                        wge(e, SE, _vc)
                        for qq in range(2):
                            e.dma_start(
                                out=w1f[:, hslot * 1024 + qq * 512:
                                        hslot * 1024 + (qq + 1) * 512]
                                .rearrange("p (c k) -> p c k", c=4),
                                in_=bass.AP(
                                    w1t,
                                    ((half * 8 + qq * 4) * 128) * (3 * D) + ec * 128,
                                    [[3 * D, 128], [128 * 3 * D, 4], [1, 128]]),
                            ).then_inc(RS[6 + hslot].h, 16)
                    p.y(fd)
                    RS[6 + hslot].reserve(32)
                    v_dma = RS[6 + hslot].n

                    def fcast(e, slot=slot, half=half, hslot=hslot, _vd=v_dma,
                              _vm=mm_v.get(slot, 0)):
                        e.wait_ge(RS[6 + hslot].h, _vd)
                        wge(e, PE, _vm)
                        e.drain()
                        e.activation(
                            w1b[:, slot * DC * 128 + half * 1024:
                                slot * DC * 128 + (half + 1) * 1024],
                            w1f[:, hslot * 1024:(hslot + 1) * 1024],
                            AF.Copy).then_inc(SE.h, 1)
                    p.s(fcast)
                    cast_v[("h", hslot)] = SE.reserve()
                cast_v[slot] = cast_v[("h", (i * 2 + 1) % 2)]

            emit_w1(0, ec_order[0])
            for i, ec in enumerate(ec_order):
                slot = i % 2
                part, sub = ec // DC, ec % DC
                par = i % 2
                pidx = tuple(par * len(TSPL) + j for j in range(len(TSPL)))

                def fmm(e, slot=slot, _vc=cast_v[slot], _ve=evac_v.get(par, 0),
                        pidx=pidx):
                    e.wait_ge(SE.h, _vc)
                    wge(e, SE, _ve)
                    ins = None
                    for j, (toff, tw) in enumerate(TSPL):
                        pt = psQ[pidx[j]]
                        for dc in range(DC):
                            ins = e.matmul(
                                pt[:, 0:tw],
                                w1b[:, slot * DC * 128 + dc * 128:
                                    slot * DC * 128 + (dc + 1) * 128],
                                xnf[:, dc * TOK + toff:dc * TOK + toff + tw],
                                start=(dc == 0), stop=(dc == DC - 1))
                    ins.then_inc(PE.h, 1)
                p.t(fmm)
                vmm = PE.reserve()
                mm_v[slot] = vmm
                if i + 1 < len(ec_order):
                    emit_w1(i + 1, ec_order[i + 1])

                if part == 2:
                    dst, doff = vch, (sub % 2) * TOK
                else:
                    dst, doff = kq, sub * TOK
                vz_wait = z_v.get(sub % 2, 0) if part == 2 else 0
                vq_wait = v_A_done if part == 0 else 0

                def fev(e, ec=ec, dst=dst, doff=doff, _vm=vmm, pidx=pidx,
                        _vz=vz_wait, _vq=vq_wait):
                    e.wait_ge(PE.h, _vm)
                    wge(e, VE, _vz)
                    wge(e, VE, _vq)
                    e.drain()
                    ins = None
                    for j, (toff, tw) in enumerate(TSPL):
                        ins = e.activation(
                            dst[:, doff + toff:doff + toff + tw],
                            psQ[pidx[j]][:, 0:tw],
                            AF.Identity, bias=b1_sb[:, ec:ec + 1])
                    ins.then_inc(SE.h, 1)
                p.s(fev)
                vev = SE.reserve()
                evac_v[par] = vev

                if part == 1 and sub == 7:
                    def frope(e, _ve=vev, _vt=v_tab):
                        e.wait_ge(SE.h, _ve)
                        e.wait_ge(DX.h, _vt)
                        rope_inplace(e, kq)
                        e.nop().then_inc(VE.h, 1)
                    p.v(frope)
                    v_kro = VE.reserve()

                if part == 1 and sub == 15:
                    v_prev_nm = 0
                    for dc in range(DC):
                        def fsq(e, dc=dc, _vk=vev, _vr=v_kro, _vp=v_prev_nm):
                            e.wait_ge(SE.h, _vk)
                            e.wait_ge(VE.h, _vr)
                            wge(e, PE, _vp)
                            e.drain()
                            e.activation(sq_scr[:, 0:TOK],
                                         kq[:, dc * TOK:(dc + 1) * TOK],
                                         AF.Square).then_inc(SE.h, 1)
                        p.s(fsq)
                        vsq = SE.reserve()

                        def fnm(e, dc=dc, _vs=vsq):
                            e.wait_ge(SE.h, _vs)
                            ins = None
                            for j, (toff, tw) in enumerate(TSPL):
                                ins = e.matmul(psN[j][:, 0:tw], ones_c[:],
                                               sq_scr[:, toff:toff + tw],
                                               start=(dc == 0), stop=(dc == DC - 1))
                            ins.then_inc(PE.h, 1)
                        p.t(fnm)
                        v_prev_nm = PE.reserve()

                    def fw(e, _vp=v_prev_nm):
                        e.wait_ge(PE.h, _vp)
                        e.drain()
                        for j, (toff, tw) in enumerate(TSPL):
                            e.tensor_scalar(nrm[:, toff:toff + tw], psN[j][:, 0:tw],
                                            1e-24, None, ALU.max)
                        e.drain()
                        e.nop().then_inc(VE.h, 1)
                    p.v(fw)
                    v_w1 = VE.reserve()

                    def fw2(e, _v=v_w1):
                        e.wait_ge(VE.h, _v)
                        e.drain()
                        e.activation(nrm[:], nrm[:], AF.Ln)
                        e.drain()
                        e.activation(nrm[:], nrm[:], AF.Exp, scale=-0.5).then_inc(SE.h, 1)
                    p.s(fw2)
                    v_w2 = SE.reserve()

                    def fw3(e, _v=v_w2):
                        e.wait_ge(SE.h, _v)
                        ins = None
                        for j, (toff, tw) in enumerate(TSPL):
                            ins = e.matmul(psW[j][:, 0:tw], ones_r[:],
                                           nrm[:, toff:toff + tw],
                                           start=True, stop=True)
                        ins.then_inc(PE.h, 1)
                    p.t(fw3)
                    v_w3 = PE.reserve()

                    def fw4(e, _v=v_w3):
                        e.wait_ge(PE.h, _v)
                        e.drain()
                        for j, (toff, tw) in enumerate(TSPL):
                            e.tensor_copy(wb_sb[:, toff:toff + tw], psW[j][:, 0:tw])
                        e.drain()
                        e.nop().then_inc(VE.h, 1)
                    p.v(fw4)
                    v_wb = VE.reserve()

                if part == 2:
                    def fz(e, sub=sub, doff=doff, _ve=vev, _vw=v_wb):
                        e.wait_ge(SE.h, _ve)
                        e.wait_ge(VE.h, _vw)
                        e.drain()
                        e.tensor_tensor(vscr[:], vch[:, doff:doff + TOK], wb_sb[:],
                                        ALU.mult)
                        e.drain()
                        e.tensor_tensor(vscr[:], vscr[:],
                                        kq[:, sub * TOK:(sub + 1) * TOK], ALU.mult)
                        e.drain()
                        e.reduce_sum(A_sb[:, sub:sub + 1], vscr[:], axis=AX)
                        e.drain()
                        e.nop().then_inc(VE.h, 1)
                    p.v(fz)
                    z_v[sub % 2] = VE.reserve()
                    if sub == DC - 1:
                        v_A_done = z_v[sub % 2]

                if part == 0 and sub == 7:
                    def frq(e, _ve=vev, _vt=v_tab):
                        e.wait_ge(SE.h, _ve)
                        e.wait_ge(DX.h, _vt)
                        rope_inplace(e, kq)
                        e.nop().then_inc(VE.h, 1)
                    p.v(frq)
                    v_qro = VE.reserve()

            v_q_all = SE.n

            def fa1(e, _va=v_A_done):
                e.wait_ge(VE.h, _va)
                e.dma_start(out=ar_i.ap(), in_=A_sb[:]).then_inc(DG.h, 16)
            p.g(fa1)
            DG.reserve(16)
            vdg1 = DG.n

            def fa2(e, _v=vdg1):
                e.wait_ge(DG.h, _v)
                e.collective_compute(
                    "AllReduce", ALU.add, replica_groups=[list(range(NCORE))],
                    ins=[ar_i.ap()], outs=[ar_o.ap()]).then_inc(CC.h, 1)
            p.g(fa2)
            CC.reserve(1)
            vcc = CC.n

            def fa3(e, _v=vcc):
                e.wait_ge(CC.h, _v)
                e.dma_start(out=A_sb[:], in_=ar_o.ap()).then_inc(DG.h, 16)
            p.g(fa3)
            DG.reserve(16)
            v_A_red = DG.n

            def fy(e, _va=v_A_red, _vq=v_qro, _vqa=v_q_all):
                e.wait_ge(DG.h, _va)
                e.wait_ge(VE.h, _vq)
                e.wait_ge(SE.h, _vqa)
                e.drain()
                for dc in range(DC):
                    e.tensor_scalar(kq[:, dc * TOK:(dc + 1) * TOK],
                                    kq[:, dc * TOK:(dc + 1) * TOK],
                                    A_sb[:, dc:dc + 1], None, ALU.mult)
                e.drain()
                e.nop().then_inc(VE.h, 1)
            p.v(fy)
            v_y = VE.reserve()
            p.run()
            barrier()
            sA2.close()

            with ExitStack() as sW:
                wtf = sW.enter_context(nc.sbuf_tensor("wtf", [128, 6 * 512], F32))
                wtb = sW.enter_context(nc.sbuf_tensor("wtb", [128, 6 * 512], BF16))
                s_scr = sW.enter_context(nc.sbuf_tensor("s_scr", [128, 512], F32))
                b2f = sW.enter_context(nc.sbuf_tensor("b2f", [1, D], F32))
                psO = [sW.enter_context(nc.psum_tensor(_nm("ps"), [128, 512], F32))
                       for _ in range(NT)]
                p = Phase(nc)

                def fb2(e):
                    e.dma_start(out=b2f[:], in_=b2r.ap()).then_inc(DX.h, 16)
                p.y(fb2)
                DX.reserve(16)
                v_b2 = DX.n
                t_cast = {}
                t_mm = {}
                v_evac = 0
                for et in range(4):
                    v_last_cast = 0
                    for dc in range(DC):
                        slot = (et * DC + dc) % 6

                        def fd(e, et=et, dc=dc, slot=slot, _vc=t_cast.get(slot, 0)):
                            wge(e, SE, _vc)
                            e.dma_start(
                                out=wtf[:, slot * 512:(slot + 1) * 512],
                                in_=w2t.ap()[dc * 128:(dc + 1) * 128,
                                             et * 512:(et + 1) * 512],
                            ).then_inc(RS[slot].h, 16)
                        p.y(fd)
                        RS[slot].reserve(16)
                        vd = RS[slot].n

                        def fc(e, slot=slot, _vd=vd, _vm=t_mm.get(slot, 0)):
                            e.wait_ge(RS[slot].h, _vd)
                            wge(e, PE, _vm)
                            e.drain()
                            e.activation(wtb[:, slot * 512:(slot + 1) * 512],
                                         wtf[:, slot * 512:(slot + 1) * 512],
                                         AF.Copy).then_inc(SE.h, 1)
                        p.s(fc)
                        t_cast[slot] = SE.reserve()
                        v_last_cast = t_cast[slot]

                        def fmm(e, et=et, dc=dc, slot=slot, _vc=t_cast[slot],
                                _vy=v_y, _ve=v_evac):
                            e.wait_ge(SE.h, _vc)
                            e.wait_ge(VE.h, _vy)
                            if dc == 0:
                                wge(e, VE, _ve)
                            ins = None
                            for tc in range(NT):
                                ins = e.matmul(
                                    psO[tc][:, 0:512],
                                    kq[:, dc * TOK + tc * 128:dc * TOK + (tc + 1) * 128],
                                    wtb[:, slot * 512:(slot + 1) * 512],
                                    start=(dc == 0), stop=False)
                            ins.then_inc(PE.h, 1)
                        p.t(fmm)
                        t_mm[slot] = PE.reserve()

                    def fbias(e, et=et, _vb=v_b2):
                        e.wait_ge(DX.h, _vb)
                        ins = None
                        for tc in range(NT):
                            ins = e.matmul(psO[tc][:, 0:512], ones_r[:],
                                           b2f[:, et * 512:(et + 1) * 512],
                                           start=False, stop=True)
                        ins.then_inc(PE.h, 1)
                    p.t(fbias)
                    vb = PE.reserve()

                    def fev(e, et=et, _vb=vb):
                        e.wait_ge(PE.h, _vb)
                        for tc in range(NT):
                            e.drain()
                            e.tensor_scalar(
                                s_scr[:],
                                x_sb[:, tc * D + et * 512:tc * D + (et + 1) * 512],
                                stat[:, 4 * NT + tc:4 * NT + tc + 1],
                                stat[:, 5 * NT + tc:5 * NT + tc + 1],
                                ALU.mult, ALU.add)
                            e.drain()
                            e.tensor_tensor(s_scr[:], s_scr[:], psO[tc][:, 0:512],
                                            ALU.add)
                            e.drain()
                            e.tensor_copy(
                                x_sb[:, tc * D + et * 512:tc * D + (et + 1) * 512],
                                s_scr[:])
                        e.drain()
                        e.nop().then_inc(VE.h, 1)
                    p.v(fev)
                    v_evac = VE.reserve()

                p.run()
            barrier()

        ffn_ctr = [0]

        def ffn_pass():
            ffn_ctr[0] += 1
            fp = ffn_ctr[0]
            with ExitStack() as sF:
                hs = sF.enter_context(nc.sbuf_tensor(_nm("hs"), [128, 64 * TH], BF16))
                wring = sF.enter_context(nc.sbuf_tensor(_nm("wring"), [128, 3 * 4096], BF16))
                gur = wring
                wdr = wring
                hb = sF.enter_context(nc.sbuf_tensor(_nm("hb"), [128, 4 * TH], BF16))
                tre = sF.enter_context(nc.sbuf_tensor(_nm("tre"), [128, 3 * TH], BF16))
                sg = sF.enter_context(nc.sbuf_tensor(_nm("sg"), [128, 2 * TH], BF16))
                psF = [sF.enter_context(nc.psum_tensor(_nm("ps"), [128, 512], F32))
                       for _ in range(8)]
                Z16 = [0] * 16

                for th in range(2):
                    fresh = (fp == 1 and th == 0)
                    p = Phase(nc)
                    if fp == 1 and th == 1:
                        def fhead(e, _f=tuple((CVW[i], CVW[i].n)
                                              for i in range(8))):
                            for ctr, val in _f:
                                e.wait_ge(ctr.h, val)
                        p.y(fhead)
                    widx = 0
                    wmm_v = {}
                    wmm_hc = {}
                    gevac_v = {}
                    spars_v = 0
                    if fresh:
                        cbuf["ring"] = wring
                        cbuf["guard_fn"] = \
                            lambda stg, pc: wmm_hc.get(stg, Z16)[4 * pc + 3]
                        conv_pump(p, 12, cap=GU_ITEMS)
                    for m in range(MQ):
                        for jp in range(2):
                            gidx = m * 2 + jp
                            b0 = 4 * (gidx % 2)
                            blk = gidx
                            slots = {}
                            waitsets = {}
                            for wi, (cache_d, kp) in enumerate(
                                    ((wgb_d, "g"), (wub_d, "u"))):
                                slot = widx % 3
                                widx += 1
                                slots[wi] = slot
                                if fresh:
                                    waitsets[wi] = ()
                                    continue

                                ohc = wmm_hc.get(slot, Z16)
                                halves = []
                                for hf in range(2):
                                    sem = RS[slot] if hf == 0 else RS[8 + slot]

                                    def fd(e, blk=blk, slot=slot, hf=hf,
                                           cache_d=cache_d, sem=sem,
                                           _vm=ohc[8 * hf + 7],
                                           _wb=wb_done[(kp, blk)]):
                                        wge(e, PE, _vm)
                                        e.wait_ge(_wb[0].h, _wb[1])
                                        e.dma_start(
                                            out=gur[:, slot * 4096 + hf * 2048:
                                                slot * 4096 + (hf + 1) * 2048],
                                            in_=cache_d.ap()[blk * 128:(blk + 1) * 128,
                                                             hf * 2048:(hf + 1) * 2048],
                                        ).then_inc(sem.h, 16)
                                    p.y(fd)
                                    sem.reserve(16)
                                    halves.append((sem, sem.n))
                                waitsets[wi] = tuple(halves)

                            hcl = []
                            for hc in range(16):
                                if fresh:
                                    fw_hc = (fresh_pc[("g", blk, hc // 4)],
                                             fresh_pc[("u", blk, hc // 4)])
                                else:
                                    fw_hc = ()

                                if fresh:
                                    hw = ()
                                elif hc in (0, 8):
                                    hw = (waitsets[0][hc // 8],
                                          waitsets[1][hc // 8])
                                else:
                                    hw = ()

                                def fmm(e, hc=hc, th=th, b0=b0,
                                        sg_=slots[0], su_=slots[1],
                                        _fw=fw_hc, _hw=hw,
                                        _vev=gevac_v.get(gidx % 2, 0)):
                                    for ctr, val in _fw:
                                        e.wait_ge(ctr.h, val)
                                    for ctr, val in _hw:
                                        e.wait_ge(ctr.h, val)
                                    if hc == 0:
                                        wge(e, VE, _vev)
                                    ins = None
                                    for jj in range(2):
                                        e.matmul(
                                            psF[b0 + jj][:, 0:TH],
                                            gur[:, sg_ * 4096 + hc * 256 + jj * 128:
                                                sg_ * 4096 + hc * 256 + (jj + 1) * 128],
                                            xnf[:, hc * TOK + th * TH:
                                                hc * TOK + (th + 1) * TH],
                                            start=(hc == 0), stop=(hc == 15))
                                        ins = e.matmul(
                                            psF[b0 + 2 + jj][:, 0:TH],
                                            gur[:, su_ * 4096 + hc * 256 + jj * 128:
                                                su_ * 4096 + hc * 256 + (jj + 1) * 128],
                                            xnf[:, hc * TOK + th * TH:
                                                hc * TOK + (th + 1) * TH],
                                            start=(hc == 0), stop=(hc == 15))
                                    ins.then_inc(PE.h, 1)
                                p.t(fmm)
                                vmm = PE.reserve()
                                hcl.append(vmm)
                            wmm_v[slots[0]] = vmm
                            wmm_v[slots[1]] = vmm
                            wmm_hc[slots[0]] = hcl
                            wmm_hc[slots[1]] = hcl
                            v_group_mm = vmm
                            if fresh:
                                conv_pump(p, 8, cap=GU_ITEMS)

                            def fsilu(e, b0=b0, jp=jp, _vm=v_group_mm,
                                      _vh=gevac_v.get((gidx - 1) % 2, 0),
                                      _vsp=spars_v):
                                e.wait_ge(PE.h, _vm)
                                wge(e, VE, _vh)
                                wge(e, VE, _vsp)
                                e.drain()
                                for jj in range(2):
                                    e.activation(sg[:, jj * TH:(jj + 1) * TH],
                                                 psF[b0 + jj][:, 0:TH],
                                                 AF.Silu).then_inc(SE.h, 1)
                            p.s(fsilu)
                            SE.reserve(2)
                            v_silu = SE.n

                            def fh(e, b0=b0, jp=jp, _vs=v_silu, _vsp=spars_v):
                                e.wait_ge(SE.h, _vs)
                                wge(e, VE, _vsp)
                                e.drain()
                                for jj in range(2):
                                    j = 2 * jp + jj
                                    e.tensor_tensor(hb[:, j * TH:(j + 1) * TH],
                                                    sg[:, jj * TH:(jj + 1) * TH],
                                                    psF[b0 + 2 + jj][:, 0:TH],
                                                    ALU.mult)
                                e.drain()
                                e.nop().then_inc(VE.h, 1)
                            p.v(fh)
                            gevac_v[gidx % 2] = VE.reserve()

                        v_h_all = gevac_v[(m * 2 + 1) % 2]

                        def fsparse(e, m=m, _vh=v_h_all):
                            e.wait_ge(VE.h, _vh)
                            h0, h1 = hb[:, 0:TH], hb[:, TH:2 * TH]
                            h2, h3 = hb[:, 2 * TH:3 * TH], hb[:, 3 * TH:4 * TH]
                            T0, T1 = sg[:, 0:TH], sg[:, TH:2 * TH]
                            M01, N01 = tre[:, 0:TH], tre[:, TH:2 * TH]
                            SEC = tre[:, 2 * TH:3 * TH]
                            e.drain()
                            e.tensor_tensor(T0, h0, h0, ALU.mult)
                            e.tensor_tensor(T1, h1, h1, ALU.mult)
                            e.drain()
                            e.tensor_tensor(M01, T0, T1, ALU.max)
                            e.tensor_tensor(N01, T0, T1, ALU.min)
                            e.drain()
                            e.tensor_tensor(T0, h2, h2, ALU.mult)
                            e.tensor_tensor(T1, h3, h3, ALU.mult)
                            e.drain()
                            e.tensor_tensor(SEC, T0, T1, ALU.max)
                            e.drain()
                            e.tensor_tensor(T0, T0, T1, ALU.min)
                            e.drain()
                            e.tensor_tensor(T1, M01, SEC, ALU.min)
                            e.tensor_tensor(T0, N01, T0, ALU.max)
                            e.drain()
                            e.tensor_tensor(SEC, T1, T0, ALU.max)
                            e.drain()
                            for j in range(4):
                                hj = hb[:, j * TH:(j + 1) * TH]
                                e.tensor_tensor(T0, hj, hj, ALU.mult)
                                e.drain()
                                e.tensor_tensor(T0, T0, SEC, ALU.is_ge)
                                e.drain()
                                e.tensor_tensor(hs[:, (4 * m + j) * TH:
                                                   (4 * m + j + 1) * TH],
                                                hj, T0, ALU.mult)
                                e.drain()
                            e.nop().then_inc(VE.h, 1)
                        p.v(fsparse)
                        spars_v = VE.reserve()
                    p.run()
                    barrier()

                    p = Phase(nc)
                    d_mm = {}
                    v_devac = 0
                    if fresh:
                        for s_ in range(3):
                            stg_wb[10 + s_] = stg_wb.get(s_ // 2, ())
                        cbuf["guard_fn"] = lambda stg, pc: d_mm.get(stg, 0)
                        conv_pump(p, 6)
                    for Ht in range(4):
                        for g4 in range(16):
                            blk = Ht * 16 + g4
                            slot = blk % 3
                            key = 10 + slot
                            if fresh:
                                ws = (fresh_pc[("d", blk, 0)],
                                      fresh_pc[("d", blk, 1)])
                            else:
                                def fd(e, blk=blk, slot=slot, key=key,
                                       _vm=d_mm.get(key, 0),
                                       _wb=wb_done[("d", blk)]):
                                    wge(e, PE, _vm)
                                    e.wait_ge(_wb[0].h, _wb[1])
                                    e.dma_start(
                                        out=wdr[:, slot * 2048:(slot + 1) * 2048],
                                        in_=wdb_d.ap()[blk * 128:(blk + 1) * 128, :],
                                    ).then_inc(RS[4 + slot].h, 16)
                                p.y(fd)
                                RS[4 + slot].reserve(16)
                                ws = ((RS[4 + slot], RS[4 + slot].n),)

                            def fmm(e, g4=g4, slot=slot, _vd=ws,
                                    _ve=v_devac):
                                for ctr, val in _vd:
                                    e.wait_ge(ctr.h, val)
                                if g4 == 0:
                                    wge(e, VE, _ve)
                                ins = None
                                for icl in range(4):
                                    ic = g4 * 4 + icl
                                    for tcl in range(NTH):
                                        ins = e.matmul(
                                            psF[tcl][:, 0:512],
                                            hs[:, ic * TH + tcl * 128:
                                               ic * TH + (tcl + 1) * 128],
                                            wdr[:, slot * 2048 + icl * 512:
                                                slot * 2048 + (icl + 1) * 512],
                                            start=(ic == 0), stop=(ic == 63))
                                ins.then_inc(PE.h, 1)
                            p.t(fmm)
                            d_mm[key] = PE.reserve()
                            if fresh:
                                conv_pump(p, 2)
                        v_dn = d_mm[key]

                        def fev(e, Ht=Ht, th=th, _vm=v_dn):
                            e.wait_ge(PE.h, _vm)
                            e.drain()
                            for tcl in range(NTH):
                                tc = th * NTH + tcl
                                e.tensor_tensor(
                                    x_sb[:, tc * D + Ht * 512:tc * D + (Ht + 1) * 512],
                                    x_sb[:, tc * D + Ht * 512:tc * D + (Ht + 1) * 512],
                                    psF[tcl][:, 0:512], ALU.add)
                                e.drain()
                            e.nop().then_inc(VE.h, 1)
                        p.v(fev)
                        v_devac = VE.reserve()
                    p.run()
                    barrier()

        if stage <= 2:
            emit_output()
            return nc
        ln_phase()
        queue_conv()
        ffn_pass()
        assert not conv_items, f"conversion queue not drained: {len(conv_items)}"
        sCV.close()
        if stage <= 3:
            emit_output()
            return nc
        cc_xg = [0, 0]
        cc_at = [0, 0, 0, 0]

        def xn3_hook(p, marks):
            for h in range(2):
                def fx1(e, h=h, _vm=marks[h]):
                    e.wait_ge(VE.h, _vm)
                    for qq in range(4):
                        e.dma_start(
                            out=bass.AP(xg_i[h], (qq * 4 * 128) * (TOK // 2),
                                        [[TOK // 2, 128],
                                         [128 * (TOK // 2), 4],
                                         [1, TOK // 2]]),
                            in_=bass.AP(xnf, qq * 4 * TOK + h * (TOK // 2),
                                        [[DC * TOK, 128], [TOK, 4],
                                         [1, TOK // 2]]),
                        ).then_inc(DG.h, 16)
                p.g(fx1)
                DG.reserve(64)
                vx = DG.n

                def fx2(e, h=h, _v=vx):
                    e.wait_ge(DG.h, _v)
                    e.collective_compute(
                        "AllGather", ALU.bypass,
                        replica_groups=[list(range(NCORE))],
                        ins=[xg_i[h].ap()],
                        outs=[xg_o[h].ap()]).then_inc(CC.h, 1)
                p.g(fx2)
                CC.reserve(1)
                cc_xg[h] = CC.n

        ln_phase(xn3_hook)

        with ExitStack() as sB:
            att_my = sB.enter_context(nc.sbuf_tensor("att_my", [128, 2 * S], BF16))
            K_my = sB.enter_context(nc.sbuf_tensor("K_my", [128, S], BF16))
            V_my = sB.enter_context(nc.sbuf_tensor("V_my", [128, S], BF16))
            Q_my = xnf


            HW = min(256, BLK)
            NHF = BLK // HW
            with ExitStack() as sQ:
                wqf = sQ.enter_context(nc.sbuf_tensor("wqf", [128, DC * 128], F32))
                wqb = sQ.enter_context(nc.sbuf_tensor("wqb", [128, DC * 512], BF16))
                xge = sQ.enter_context(nc.sbuf_tensor("xge", [128, 2 * DC * HW], BF16))
                vsc = sQ.enter_context(nc.sbuf_tensor("vsc", [128, HW], BF16))
                psB = [sQ.enter_context(nc.psum_tensor(_nm("ps"), [128, HW], F32))
                       for _ in range(4)]
                psT = sQ.enter_context(nc.psum_tensor(_nm("ps"), [128, 512], F32))

                p = Phase(nc)

                v_wqb = 0
                gvals = {}
                for oc in range(4):
                    def fwq(e, oc=oc, _vp=v_wqb):
                        if "qb" not in gvals:
                            pid = nc.partition_id()
                            gvals["qb"] = e.compute_val((pid // 2) * 768)
                            gvals["qp"] = e.compute_val(
                                gvals["qb"] + (pid % 2) * 256)
                        if oc < 2:
                            off = e.compute_val(gvals["qp"] + oc * 128)
                        else:
                            off = e.compute_val(gvals["qb"] + 512 + (oc - 2) * 128)
                        wge(e, SE, _vp)
                        e.dma_start(
                            out=wqf[:].rearrange("p (c k) -> p c k", c=DC),
                            in_=bass.AP(wqt, off,
                                        [[3072, 128], [128 * 3072, DC], [1, 128]]),
                        ).then_inc(RG[2].h, 16)
                    p.g(fwq)
                    RG[2].reserve(16)
                    v_wq = RG[2].n

                    def fwqc(e, oc=oc, _v=v_wq):
                        e.wait_ge(RG[2].h, _v)
                        e.drain()
                        e.activation(wqb[:, oc * DC * 128:(oc + 1) * DC * 128],
                                     wqf[:], AF.Copy).then_inc(SE.h, 1)
                    p.s(fwqc)
                    v_wqb = SE.reserve()

                xld_v = {}
                bmm_v = {}
                bev_v = {}
                tr_v = 0
                vev_prev = [0]
                for b in range(16):
                    for hf in range(NHF):
                        islot = (b * NHF + hf) % 2
                        rnk = b if b < 8 else 15 - b
                        loc = (0 if b < 8 else BLK) + 0
                        colo = loc + hf * HW

                        hh = 0 if b < 8 else 1
                        colo2 = colo - hh * (TOK // 2)

                        def mk_fxl(qlo, qhi, sem):
                            def fxl(e, rnk=rnk, colo2=colo2, islot=islot, hh=hh,
                                    qlo=qlo, qhi=qhi, sem=sem,
                                    _vm=bmm_v.get(islot, 0), _vg=cc_xg[hh]):
                                e.wait_ge(CC.h, _vg)
                                wge(e, PE, _vm)
                                for qq in range(qlo, qhi):
                                    e.dma_start(
                                        out=xge[:, islot * DC * HW + qq * 4 * HW:
                                                islot * DC * HW + (qq + 1) * 4 * HW]
                                        .rearrange("p (c t) -> p c t", c=4),
                                        in_=bass.AP(
                                            xg_o[hh],
                                            (rnk * D + qq * 4 * 128) * (TOK // 2)
                                            + colo2,
                                            [[TOK // 2, 128],
                                             [128 * (TOK // 2), 4], [1, HW]]),
                                    ).then_inc(sem.h, 16)
                            return fxl
                        p.y(mk_fxl(0, 2, RS[islot]))
                        RS[islot].reserve(32)
                        p.g(mk_fxl(2, 4, RG[islot]))
                        RG[islot].reserve(32)
                        v_xl = (RS[islot], RS[islot].n, RG[islot], RG[islot].n)

                        par = 0
                        pb0 = 0

                        def fbm(e, islot=islot, pb0=pb0, _vx=v_xl, _vw=v_wqb,
                                _ve=bev_v.get(par, 0)):
                            e.wait_ge(_vx[0].h, _vx[1])
                            e.wait_ge(_vx[2].h, _vx[3])
                            e.wait_ge(SE.h, _vw)
                            wge(e, SE, _ve)
                            ins = None
                            for oc in range(4):
                                for dc in range(DC):
                                    ins = e.matmul(
                                        psB[pb0 + oc][:, 0:HW],
                                        wqb[:, oc * DC * 128 + dc * 128:
                                            oc * DC * 128 + (dc + 1) * 128],
                                        xge[:, islot * DC * HW + dc * HW:
                                            islot * DC * HW + (dc + 1) * HW],
                                        start=(dc == 0), stop=(dc == DC - 1))
                            ins.then_inc(PE.h, 1)
                        p.t(fbm)
                        bmm_v[islot] = PE.reserve()

                        gcol = b * BLK + hf * HW

                        def fbe(e, pb0=pb0, gcol=gcol, _vm=bmm_v[islot], _vt=tr_v):
                            e.wait_ge(PE.h, _vm)
                            wge(e, PE, _vt)
                            e.drain()
                            e.activation(Q_my[:, 0 * S + gcol:0 * S + gcol + HW],
                                         psB[pb0 + 0][:, 0:HW], AF.Copy)
                            e.activation(Q_my[:, 1 * S + gcol:1 * S + gcol + HW],
                                         psB[pb0 + 1][:, 0:HW], AF.Copy)
                            e.activation(K_my[:, gcol:gcol + HW],
                                         psB[pb0 + 2][:, 0:HW], AF.Copy)
                            e.activation(vsc[:], psB[pb0 + 3][:, 0:HW], AF.Copy)
                            e.drain()
                            e.nop().then_inc(SE.h, 1)
                        p.s(fbe)
                        bev_v[par] = SE.reserve()

                        def fbt(e, gcol=gcol, _ve=bev_v[par], _vp=vev_prev[0]):
                            e.wait_ge(SE.h, _ve)
                            wge(e, VE, _vp)
                            pb = psT.ap().bitcast(BF16)
                            ins = None
                            for k in range(HW // 128):
                                ins = e.transpose(pb[:, k * 128:(k + 1) * 128],
                                                  vsc[:, k * 128:(k + 1) * 128],
                                                  ident[:])
                            ins.then_inc(PE.h, 1)
                        p.t(fbt)
                        tr_v = PE.reserve()

                        def fbv(e, gcol=gcol, _vt=tr_v):
                            e.wait_ge(PE.h, _vt)
                            pb = psT.ap().bitcast(BF16)
                            e.drain()
                            e.tensor_copy(V_my[:, gcol:gcol + HW], pb[:, 0:HW])
                            e.drain()
                            e.nop().then_inc(VE.h, 1)
                        p.v(fbv)
                        vev_prev[0] = VE.reserve()
                p.run()
                barrier()

            with ExitStack() as sT:
                pt_sb = sT.enter_context(nc.sbuf_tensor("pt_sb", [128, 3 * 2 * QC], BF16))
                mask_sb = sT.enter_context(
                    nc.sbuf_tensor("mask_sb", [128, NPAT * 2 * QC], BF16))
                recs = sT.enter_context(nc.sbuf_tensor("recs", [1, 2 * QC], F32))
                recb = sT.enter_context(nc.sbuf_tensor("recb", [128, 2 * QC], BF16))
                psOU = [sT.enter_context(nc.psum_tensor(_nm("ps"), [128, 2 * QC], F32))
                        for _ in range(2)]
                psSG = [sT.enter_context(nc.psum_tensor(_nm("ps"), [1, 2 * QC], F32))
                        for _ in range(2)]
                psST = [sT.enter_context(nc.psum_tensor(_nm("ps"), [128, 2 * QC], F32))
                        for _ in range(3)]
                psRC = sT.enter_context(nc.psum_tensor(_nm("ps"), [128, 2 * QC], F32))

                p = Phase(nc)

                def fmload(e):
                    e.dma_start(out=mask_sb[:], in_=mask_d.ap()).then_inc(DX.h, 16)
                p.y(fmload)
                DX.reserve(16)
                v_mask = DX.n
                exp_v = {}
                cons_v = {}
                mask_v = {}
                v_att_ev = 0
                ktg = 0
                pend = None

                def emit_st(kt, qc, ktg):
                    qpos = qc * QC

                    def f(e, kt=kt, qpos=qpos, ktg=ktg,
                          _vex=exp_v.get(ktg % 3, 0)):
                        wge(e, SE, _vex)
                        e.matmul(
                            psST[ktg % 3][:, 0:2 * QC],
                            K_my[:, kt * 128:(kt + 1) * 128],
                            bass.AP(Q_my, qpos, [[2 * S, 128], [S, 2], [1, QC]]),
                            start=True, stop=True).then_inc(PE.h, 1)
                    p.t(f)
                    return PE.reserve()

                def emit_exp(ktg, vst, diag_ri):
                    def f(e, ktg=ktg, _vst=vst, _vcons=cons_v.get(ktg % 3, 0)):
                        e.wait_ge(PE.h, _vst)
                        wge(e, PE, _vcons)
                        e.drain()
                        e.activation(pt_sb[:, (ktg % 3) * 2 * QC:(ktg % 3 + 1) * 2 * QC],
                                     psST[ktg % 3][:, 0:2 * QC], AF.Exp,
                                     bias=expb[:, :], scale=INV_SQRT_HD).then_inc(SE.h, 1)
                    p.s(f)
                    exp_v[ktg % 3] = SE.reserve()
                    if diag_ri is not None:
                        def fm(e, ktg=ktg, ri=diag_ri, _ve=exp_v[ktg % 3],
                               _vm=v_mask):
                            e.wait_ge(SE.h, _ve)
                            e.wait_ge(DX.h, _vm)
                            e.drain()
                            e.tensor_tensor(
                                pt_sb[:, (ktg % 3) * 2 * QC:(ktg % 3 + 1) * 2 * QC],
                                pt_sb[:, (ktg % 3) * 2 * QC:(ktg % 3 + 1) * 2 * QC],
                                mask_sb[:, ri * 2 * QC:(ri + 1) * 2 * QC], ALU.mult)
                            e.drain()
                            e.nop().then_inc(VE.h, 1)
                        p.v(fm)
                        mask_v[ktg] = VE.reserve()

                def emit_sigpv(kt, qc, nkt, ktg):
                    bk = qc % 2

                    def f(e, kt=kt, ktg=ktg, bk=bk, first=(kt == 0),
                          last=(kt == nkt - 1),
                          _vex=exp_v[ktg % 3], _vm=mask_v.get(ktg, 0),
                          _vae=att_ev[bk] if kt == 0 else 0):
                        e.wait_ge(SE.h, _vex)
                        wge(e, VE, _vm)
                        if first:
                            wge(e, VE, _vae)
                        pt = pt_sb[:, (ktg % 3) * 2 * QC:(ktg % 3 + 1) * 2 * QC]
                        e.matmul(psSG[bk][:, 0:2 * QC], ones_c[:], pt,
                                 start=first, stop=last)
                        e.matmul(psOU[bk][:, 0:2 * QC],
                                 V_my[:, kt * 128:(kt + 1) * 128],
                                 pt, start=first, stop=last).then_inc(PE.h, 1)
                    p.t(f)
                    cons_v[ktg % 3] = PE.reserve()
                    return cons_v[ktg % 3]

                att_ev = {0: 0, 1: 0}
                pending_norm = [None]

                def emit_norm():
                    if pending_norm[0] is None:
                        return
                    qc_, qpos_, v_last_ = pending_norm[0]
                    pending_norm[0] = None
                    bk = qc_ % 2

                    def fr1(e, _vl=v_last_, bk=bk):
                        e.wait_ge(PE.h, _vl)
                        e.drain()
                        e.reciprocal(recs[:], psSG[bk][:, 0:2 * QC])
                        e.drain()
                        e.nop().then_inc(VE.h, 1)
                    p.v(fr1)
                    v_rec = VE.reserve()

                    def fr2(e, _v=v_rec):
                        e.wait_ge(VE.h, _v)
                        e.matmul(psRC[:, 0:2 * QC], ones_r[:], recs[:],
                                 start=True, stop=True).then_inc(PE.h, 1)
                    p.t(fr2)
                    v_rcb = PE.reserve()

                    def fr3(e, qpos=qpos_, bk=bk, _v=v_rcb):
                        e.wait_ge(PE.h, _v)
                        e.drain()
                        e.tensor_copy(recb[:], psRC[:, 0:2 * QC])
                        e.drain()
                        for hp in range(2):
                            e.tensor_tensor(
                                att_my[:, hp * S + qpos:hp * S + qpos + QC],
                                psOU[bk][:, hp * QC:(hp + 1) * QC],
                                recb[:, hp * QC:(hp + 1) * QC], ALU.mult)
                        e.drain()
                        e.nop().then_inc(VE.h, 1)
                    p.v(fr3)
                    att_ev[bk] = VE.reserve()

                    if (qc_ + 1) % (NQCG // 4) == 0 and qc_ != NQCG - 1:
                        ch = (qc_ + 1) // (NQCG // 4) - 1
                        S4 = S // 4

                        def fg1a(e, ch=ch, S4=S4, _v=att_ev[bk]):
                            e.wait_ge(VE.h, _v)
                            for hp in range(2):
                                e.dma_start(
                                    out=at_i[ch].ap()[hp * 128:(hp + 1) * 128, :],
                                    in_=att_my[:, hp * S + ch * S4:
                                               hp * S + (ch + 1) * S4],
                                ).then_inc(DG.h, 16)
                        p.g(fg1a)
                        DG.reserve(32)
                        vga = DG.n

                        def fg2a(e, ch=ch, _v=vga):
                            e.wait_ge(DG.h, _v)
                            e.collective_compute(
                                "AllGather", ALU.bypass,
                                replica_groups=[list(range(NCORE))],
                                ins=[at_i[ch].ap()],
                                outs=[at_og.ap()[ch * NCORE * 256:
                                                 (ch + 1) * NCORE * 256, :]],
                            ).then_inc(CC.h, 1)
                        p.g(fg2a)
                        CC.reserve(1)
                        cc_at[ch] = CC.n

                pend_q = []
                for qc in range(NQCG):
                    qpos = qc * QC
                    nkt = (qpos + QC) // 128
                    d0 = qpos // 128
                    for kt in range(nkt):
                        vst = emit_st(kt, qc, ktg)
                        if len(pend_q) >= 2:
                            emit_sigpv(*pend_q.pop(0))
                        emit_exp(ktg, vst, kt - d0 if kt >= d0 else None)
                        pend_q.append((kt, qc, nkt, ktg))
                        ktg += 1
                        if kt == 1:
                            emit_norm()
                    while pend_q:
                        v_last = emit_sigpv(*pend_q.pop(0))
                    pending_norm[0] = (qc, qpos, v_last)
                emit_norm()
                p.run()
                barrier()

            p = Phase(nc)

            S4 = S // 4

            def fg1(e, S4=S4):
                for hp in range(2):
                    e.dma_start(out=at_i[3].ap()[hp * 128:(hp + 1) * 128, :],
                                in_=att_my[:, hp * S + 3 * S4:(hp + 1) * S]
                                ).then_inc(DG.h, 16)
            p.g(fg1)
            DG.reserve(32)
            vg1 = DG.n

            def fg2(e, _v=vg1):
                e.wait_ge(DG.h, _v)
                e.collective_compute(
                    "AllGather", ALU.bypass, replica_groups=[list(range(NCORE))],
                    ins=[at_i[3].ap()],
                    outs=[at_og.ap()[3 * NCORE * 256:4 * NCORE * 256, :]],
                ).then_inc(CC.h, 1)
            p.g(fg2)
            CC.reserve(1)
            cc_at[3] = CC.n

            CHK = NCORE * 256 * S4

            g3v = {}

            def fg3a(e, S4=S4, _v=cc_at[3]):
                e.wait_ge(CC.h, _v)
                pid = nc.partition_id()
                of0 = e.compute_val((pid // 4) * CHK + (pid % 4) * BLK)
                g3v["of0"] = of0
                e.dma_start(
                    out=bass.AP(Q_my, 0,
                                [[DC * TOK, 128], [8 * TOK, 2],
                                 [TOK, 8], [1, BLK]]),
                    in_=bass.AP(at_og, of0,
                                [[S4, 128], [8 * 128 * S4, 2],
                                 [128 * S4, 8], [1, BLK]]),
                ).then_inc(DG.h, 16)
            p.g(fg3a)
            DG.reserve(16)

            def fg3b(e, S4=S4):
                of1 = e.compute_val(3 * CHK + (S4 - BLK) - g3v["of0"])
                e.dma_start(
                    out=bass.AP(Q_my, BLK,
                                [[DC * TOK, 128], [8 * TOK, 2],
                                 [TOK, 8], [1, BLK]]),
                    in_=bass.AP(at_og, of1,
                                [[S4, 128], [8 * 128 * S4, 2],
                                 [128 * S4, 8], [1, BLK]]),
                ).then_inc(DG.h, 16)
            p.g(fg3b)
            DG.reserve(16)
            v_att_in = DG.n

            def fg4(e, _v=v_att_in):
                e.wait_ge(DG.h, _v)
            p.g(fg4)
            p.run()
            barrier()

            with ExitStack() as sP:
                wtf = sP.enter_context(nc.sbuf_tensor("wpf", [128, 6 * 512], F32))
                wtb = sP.enter_context(nc.sbuf_tensor("wpb", [128, 6 * 512], BF16))
                psO = [sP.enter_context(nc.psum_tensor(_nm("ps"), [128, 512], F32))
                       for _ in range(NT)]
                p = Phase(nc)
                t_cast = {}
                t_mm = {}
                v_evac = 0
                for et in range(4):
                    for ec in range(DC):
                        slot = (et * DC + ec) % 6

                        def fd(e, et=et, ec=ec, slot=slot, _vc=t_cast.get(slot, 0)):
                            wge(e, SE, _vc)
                            e.dma_start(
                                out=wtf[:, slot * 512:(slot + 1) * 512],
                                in_=wpt.ap()[ec * 128:(ec + 1) * 128,
                                             et * 512:(et + 1) * 512],
                            ).then_inc(RS[slot].h, 16)
                        p.y(fd)
                        RS[slot].reserve(16)
                        vd = RS[slot].n

                        def fc(e, slot=slot, _vd=vd, _vm=t_mm.get(slot, 0)):
                            e.wait_ge(RS[slot].h, _vd)
                            wge(e, PE, _vm)
                            e.drain()
                            e.activation(wtb[:, slot * 512:(slot + 1) * 512],
                                         wtf[:, slot * 512:(slot + 1) * 512],
                                         AF.Copy).then_inc(SE.h, 1)
                        p.s(fc)
                        t_cast[slot] = SE.reserve()

                        def fmm(e, et=et, ec=ec, slot=slot, _vc=t_cast[slot],
                                _ve=v_evac):
                            e.wait_ge(SE.h, _vc)
                            if ec == 0:
                                wge(e, VE, _ve)
                            ins = None
                            for tc in range(NT):
                                ins = e.matmul(
                                    psO[tc][:, 0:512],
                                    Q_my[:, ec * TOK + tc * 128:ec * TOK + (tc + 1) * 128],
                                    wtb[:, slot * 512:(slot + 1) * 512],
                                    start=(ec == 0), stop=(ec == DC - 1))
                            ins.then_inc(PE.h, 1)
                        p.t(fmm)
                        t_mm[slot] = PE.reserve()
                    v_pj = t_mm[slot]

                    def fev(e, et=et, _vm=v_pj):
                        e.wait_ge(PE.h, _vm)
                        e.drain()
                        for tc in range(NT):
                            e.tensor_tensor(
                                x_sb[:, tc * D + et * 512:tc * D + (et + 1) * 512],
                                x_sb[:, tc * D + et * 512:tc * D + (et + 1) * 512],
                                psO[tc][:, 0:512], ALU.add)
                            e.drain()
                        e.nop().then_inc(VE.h, 1)
                    p.v(fev)
                    v_evac = VE.reserve()
                p.run()
            barrier()

        if stage <= 4:
            emit_output()
            return nc
        ln_phase()
        ffn_pass()

        emit_output()
    return nc


def _rope_perm():
    return np.concatenate([np.arange(0, 1024, 2), np.arange(1, 1024, 2),
                           np.arange(1024, 2048)])


def _ic_perm():
    perm = np.empty(I_FFN, dtype=np.int64)
    for m in range(MQ):
        for j in range(4):
            for pp in range(128):
                perm[(4 * m + j) * 128 + pp] = 512 * m + 4 * pp + j
    return perm


def prep_inputs(kw, BLK=512, QC=256):
    S = 16 * BLK
    TOK = 2 * BLK
    NPAT = QC // 128
    pi = _rope_perm()
    pI = _ic_perm()

    X = np.asarray(kw["L0_attn_a_X"], dtype=np.float32)[0]
    W_in = np.asarray(kw["L0_attn_a_W_in"], dtype=np.float32)
    b_in = np.asarray(kw["L0_attn_a_b_in"], dtype=np.float32)
    W_out = np.asarray(kw["L0_attn_a_W_out"], dtype=np.float32)
    b_out = np.asarray(kw["L0_attn_a_b_out"], dtype=np.float32)
    Wqkv = np.asarray(kw["L0_attn_b_weight_qkv"], dtype=np.float32)
    Wproj = np.asarray(kw["L0_attn_b_weight_proj"], dtype=np.float32)
    wg = np.asarray(kw["L0_ffn_w_gate"], dtype=np.float32)
    wu = np.asarray(kw["L0_ffn_w_up"], dtype=np.float32)
    wd = np.asarray(kw["L0_ffn_w_down"], dtype=np.float32)

    w1 = W_in.reshape(3, D, D)[:, pi, :].reshape(3 * D, D)
    w1t = np.ascontiguousarray(w1.T)
    b1p = b_in.reshape(3, D)[:, pi].reshape(3 * D)
    b1c = np.ascontiguousarray(b1p.reshape(3 * DC, 128).T)
    w2t = np.ascontiguousarray(W_out[:, pi].T)
    b2r = np.ascontiguousarray(b_out.reshape(1, D))
    wqt = np.ascontiguousarray(Wqkv.T)
    wpt = np.ascontiguousarray(Wproj.T)
    wgp = np.ascontiguousarray(wg[:, pI])
    wup = np.ascontiguousarray(wu[:, pI])
    wdp = np.ascontiguousarray(wd[pI, :])

    def tile_gu(w):
        out = np.empty((32 * 128, 4096), np.float32)
        for b in range(32):
            m_, jp_ = b // 2, b % 2
            blkc = w[:, (4 * m_ + 2 * jp_) * 128:(4 * m_ + 2 * jp_ + 2) * 128]
            out[b * 128:(b + 1) * 128] = (
                blkc.reshape(16, 128, 256).transpose(1, 0, 2).reshape(128, 4096))
        return np.ascontiguousarray(out)

    def tile_wd(w):
        out = np.empty((64 * 128, 2048), np.float32)
        for b in range(64):
            Ht_, g4_ = b // 16, b % 16
            blkc = w[g4_ * 512:(g4_ + 1) * 512, Ht_ * 512:(Ht_ + 1) * 512]
            out[b * 128:(b + 1) * 128] = (
                blkc.reshape(4, 128, 512).transpose(1, 0, 2).reshape(128, 2048))
        return np.ascontiguousarray(out)

    wgp = tile_gu(wgp)
    wup = tile_gu(wup)
    wdp = tile_wd(wdp)

    ident = np.eye(128, dtype=BF)

    mask = np.zeros((128, NPAT * 2 * QC), dtype=BF)
    for ri in range(NPAT):
        kk = np.arange(128)[:, None] + ri * 128
        qq = np.arange(QC)[None, :]
        m = (kk <= qq).astype(np.float32).astype(BF)
        for hp in range(2):
            mask[:, ri * 2 * QC + hp * QC:(ri * 2) * QC + (hp + 1) * QC] = m

    inv = (10000.0 ** (-np.arange(512, dtype=np.float64) * 2.0 / 1024.0))

    in_maps = []
    shard_rows = []
    for c in range(NCORE):
        blo = np.arange(c * BLK, (c + 1) * BLK)
        bhi = np.arange(S - (c + 1) * BLK, S - c * BLK)
        rows = np.concatenate([blo, bhi])
        shard_rows.append(rows)
        pos = rows.astype(np.float64)
        ang = pos[None, :] * inv[:, None]
        cosd = np.cos(ang).astype(np.float32).astype(BF)
        sind = np.sin(ang).astype(np.float32).astype(BF)
        im = {
            "x_in": np.ascontiguousarray(X[rows]),
            "w1t": w1t, "b1c": b1c, "w2t": w2t, "b2r": b2r,
            "wqt": wqt, "wpt": wpt,
            "wg_d": wgp, "wu_d": wup, "wd_d": wdp,
            "cos_d": cosd, "sin_d": sind,
            "mask_d": mask, "ident_d": ident,
        }
        in_maps.append(im)
    return in_maps, shard_rows


def kernel(**kw):
    BLK, QC = 512, 256
    S = 16 * BLK
    in_maps, shard_rows = prep_inputs(kw, BLK, QC)
    nc = build_program(BLK, QC)
    res = run_bass_kernel_spmd(nc, in_maps, core_ids=list(range(NCORE)))
    out = np.zeros((1, S, D), dtype=np.float32)
    for c in range(NCORE):
        out[0, shard_rows[c]] = res.results[c]["out"]
    return out
